# version 17
# baseline (speedup 1.0000x reference)
import numpy as np
import concourse.bass as bass
import concourse.mybir as mybir
from concourse.bass_utils import run_bass_kernel_spmd

F32 = mybir.dt.float32
FP8 = mybir.dt.float8e4
Alu = mybir.AluOpType
Act = mybir.ActivationFunctionType
AxX = mybir.AxisListType.X

SC_SCALE = float(128.0 ** 0.5)
BETA = 2.5
PRE_BIAS = -20.0
MARGIN_RAW = 20.0 / SC_SCALE
EPS = 1e-5
LAMBDA_INIT = 0.8 - 0.6 * float(np.exp(-0.3 * 12))
BANK = 512


def _al(x, a=32):
    return (x + a - 1) // a * a


class Plan:
    ENGINES = ("sync", "gpsimd", "tensor", "scalar", "vector")

    def __init__(self):
        self.ops = {e: [] for e in self.ENGINES}
        self.cnt = {}
        self.last_wait = {e: {} for e in self.ENGINES}

    def sem(self, name):
        self.cnt.setdefault(name, 0)

    def emit(self, eng, fn, waits=(), inc=None, inc_n=1):
        w = []
        for s, v in waits:
            assert v <= self.cnt[s], ("future wait", eng, s, v, self.cnt[s])
            if self.last_wait[eng].get(s, -1) < v:
                self.last_wait[eng][s] = v
                w.append((s, v))
        self.ops[eng].append((fn, w, inc, inc_n))
        if inc is not None:
            self.cnt[inc] += inc_n
            return self.cnt[inc]
        return None

    def drain(self, eng):
        self.ops[eng].append(("drain", [], None, 0))


def build_program(NT=16, NH=16, debug_scratch=False, use_lse=True):
    NE = NH
    T, E = 128 * NT, 128 * NE
    TQC = min(4, NT)
    NCH = NT // TQC
    GR = min(4, NT)
    NGR = NT // GR
    WG = 1
    NWG = NE // WG
    NHALF = 2 if NT >= 2 else 1
    HTT = NT // NHALF
    PW = min(1024, T)
    NSC2 = T // PW
    PB = PW // 512
    ECH = max(1, E // 512)

    nc = bass.Bass("TRN2", target_bir_lowering=False)

    d_xT = nc.dram_tensor("xT", [NE, 128, T], F32, kind="ExternalInput").ap()
    d_x = nc.dram_tensor("x", [NT, 128, E], F32, kind="ExternalInput").ap()
    d_wqkv = nc.dram_tensor("wqkv", [NH, 128, 3 * E], F32, kind="ExternalInput").ap()
    d_wout = nc.dram_tensor("wout", [NE, 128, E], F32, kind="ExternalInput").ap()
    d_w2 = nc.dram_tensor("w2", [NE, 128, E], F32, kind="ExternalInput").ap()
    d_cos = nc.dram_tensor("cos", [128, NT * 64], F32, kind="ExternalInput").ap()
    d_sin = nc.dram_tensor("sin", [128, NT * 64], F32, kind="ExternalInput").ap()
    d_ident = nc.dram_tensor("ident", [128, 128], F32, kind="ExternalInput").ap()
    d_subln = nc.dram_tensor("subln", [128, 128], F32, kind="ExternalInput").ap()
    d_lam = nc.dram_tensor("lam", [128, 256], F32, kind="ExternalInput").ap()
    d_scl = nc.dram_tensor("scl", [128, 2], F32, kind="ExternalInput").ap()
    d_hsel = nc.dram_tensor("hsel", [128, 2], F32, kind="ExternalInput").ap()
    d_out = nc.dram_tensor("out", [NT, 128, E], F32, kind="ExternalOutput").ap()
    skind = "ExternalOutput" if debug_scratch else "Internal"
    d_dbg2 = (nc.dram_tensor("dbg2", [8, 128, 9 * T], F32, kind="ExternalOutput").ap()
              if debug_scratch else None)
    d_qT = nc.dram_tensor("scr_qT", [NH, 65, 2 * T], F32, kind=skind).ap()
    d_kT = nc.dram_tensor("scr_kT", [NH, 64, 2 * T], F32, kind=skind).ap()
    d_v = nc.dram_tensor("scr_v", [NH, 128, T], F32, kind=skind).ap()
    d_aT = nc.dram_tensor("scr_aT", [NE, 128, T], F32, kind=skind).ap()

    avail = (nc.sbuf_bytes_remaining - 512) // 32 * 32
    guard = nc.sbuf_tensor("arena", [128, avail], mybir.dt.uint8)
    big = guard.__enter__()
    nc._guards = [guard]

    def carve(off, cols, dt=F32):
        nb = cols * mybir.dt.size(dt)
        assert off % 32 == 0 and off + nb <= avail, ("overflow", off, nb, avail)
        return big[:, off : off + nb].bitcast(dt)

    class Bump:
        def __init__(self, start):
            self.off = start

        def take(self, cols, dt=F32):
            ap = carve(self.off, cols, dt)
            self.off = _al(self.off + cols * mybir.dt.size(dt))
            return ap

    cb = Bump(0)
    c_ident = cb.take(128)
    c_subln = cb.take(128)
    c_lam = cb.take(8)
    c_scl = cb.take(2)
    c_cst = cb.take(4)
    c_hsel = cb.take(2)
    c_one = cb.take(1)
    const_end = _al(cb.off)

    p1 = Bump(const_end)
    s_xT = p1.take(NE * T)
    s_wq = [p1.take(3 * E) for _ in range(2)]
    s_gb = [p1.take(3 * GR * 128) for _ in range(2)]
    s_spq = [p1.take(2 * GR * 128) for _ in range(2)]
    s_tmp = [p1.take(GR * 64) for _ in range(2)]
    s_cos = p1.take(NT * 64)
    s_sin = p1.take(NT * 64)
    assert p1.off <= avail, ("P1 overflow", p1.off, avail)

    p2 = Bump(const_end)
    s_qk = [p2.take(4 * T) for _ in range(2)]
    s_va = [p2.take(NT * 129) for _ in range(2)]
    s_E = [p2.take(2 * TQC * 128) for _ in range(3)]
    s_at = p2.take(NT * 128)
    s_aT = [p2.take(T) for _ in range(2)]
    s_ub = p2.take(9 * 129)
    s_uT = p2.take(2 * 4 * 128)
    s_zacc = [p2.take(2 * 4 * 128) for _ in range(2)]
    s_zw = p2.take(2 * 4 * 128)
    s_as = p2.take(TQC * 128)
    s_as2 = p2.take(TQC * 128)
    s_st = p2.take(64)
    s_zb = p2.take(NT * 4)
    s_zc = p2.take(NT * 2)
    s_zd = p2.take(NT * 2)
    s_dt = p2.take(max(16, 2 * NT))
    s_dstage = p2.take(128)
    s_qkbf = p2.take(4 * T, mybir.dt.bfloat16)
    assert p2.off <= avail, ("P2 overflow", p2.off, avail)
    s_lamrep = carve(s_gb[0].offset * 4, 256)
    s_lsc = carve(s_gb[0].offset * 4 + 1024, 64)

    p3 = Bump(const_end)
    s_y = p3.take(NT * E)
    s_d2 = p3.take(NE * E, FP8)
    p3b = Bump(p3.off)
    s_ga = [p3b.take(T) for _ in range(2)]
    s_gw = [p3b.take(E) for _ in range(2)]
    s_w2b = p3b.take(E)
    p3c = Bump(p3.off)
    s_d1 = p3c.take(NE * HTT * 128, FP8)
    s_sq = p3c.take(E)
    s_st3 = p3c.take(32)
    p3d = Bump(max(p3b.off, p3c.off))
    s_sq1 = p3d.take(E // 2)
    s_st1 = p3d.take(32)
    assert p3d.off <= avail, ("P3 overflow", p3b.off, p3c.off, p3d.off)

    pg = nc.psum_tensor("pbig", [128, 8 * BANK], F32)
    pb = pg.__enter__()
    nc._guards.append(pg)

    def bank(i):
        return pb[:, i * BANK : (i + 1) * BANK]

    P = Plan()
    for s in ["IN", "W0", "W1", "Q0", "Q1", "SQ0", "SQ1", "SV0", "SV1",
              "SA0", "SA1", "G00", "G01", "G10", "G11", "W2S",
              "Y0", "Y1", "Y2", "Y3", "ST", "SD",
              "PE", "ACT", "DVE"]:
        P.sem(s)

    def din(fn, waits=(), sem="IN"):
        return P.emit("sync", fn, waits, sem, 16)

    def dout(fn, waits=(), sem="ST"):
        return P.emit("gpsimd", fn, waits, sem, 16)

    def pe(fn, waits=(), inc=True):
        return P.emit("tensor", fn, waits, "PE" if inc else None)

    def act(fn, waits=(), inc=True):
        return P.emit("scalar", fn, waits, "ACT" if inc else None)

    def dve(fn, waits=(), inc=True):
        return P.emit("vector", fn, waits, "DVE" if inc else None)

    nc.const_aps.register(F32, EPS, c_cst[:, 1:2])
    nc.const_aps.register(F32, PRE_BIAS, c_cst[:, 0:1])
    nc.const_aps.register(F32, 1e-30, c_cst[:, 2:3])
    dve(lambda e: e.memset(c_cst[:, 1:2], EPS))
    dve(lambda e: e.memset(c_cst[:, 0:1], PRE_BIAS))
    dve(lambda e: e.memset(c_cst[:, 2:3], 1e-30))
    P.drain("vector")
    din(lambda e: e.dma_start(c_ident, d_ident))
    din(lambda e: e.dma_start(c_subln, d_subln))
    din(lambda e: e.dma_start(s_lamrep, d_lam))
    din(lambda e: e.dma_start(c_scl, d_scl))
    din(lambda e: e.dma_start(c_hsel, d_hsel))
    din(lambda e: e.dma_start(s_cos, d_cos))
    din(lambda e: e.dma_start(s_sin, d_sin))
    for i in range(NE):
        din(lambda e, i=i: e.dma_start(s_xT[:, i * T : (i + 1) * T], d_xT[i]))
    ld_all = P.cnt["IN"]

    dve(lambda e: e.tensor_mul(s_lsc, s_lamrep[:, 0:64], s_lamrep[:, 64:128]),
        [("IN", ld_all)])
    P.drain("vector")
    dve(lambda e: e.reduce_sum(c_lam[:, 1:2], s_lsc, AxX))
    P.drain("vector")
    dve(lambda e: e.tensor_mul(s_lsc, s_lamrep[:, 128:192], s_lamrep[:, 192:256]))
    P.drain("vector")
    v_l = dve(lambda e: e.reduce_sum(c_lam[:, 2:3], s_lsc, AxX))
    dve(lambda e: e.reciprocal(c_lam[:, 7:8], c_scl[:, 0:1]))
    dve(lambda e: e.reciprocal(c_lam[:, 3:4], c_scl[:, 1:2]))
    P.drain("vector")
    v_rdq = dve(lambda e: e.tensor_mul(c_lam[:, 6:7], c_lam[:, 7:8], c_lam[:, 3:4]))
    a_l1 = act(lambda e: e.activation(c_lam[:, 4:5], c_lam[:, 1:2], Act.Exp),
               [("DVE", v_l)])
    a_l2 = act(lambda e: e.activation(c_lam[:, 5:6], c_lam[:, 2:3], Act.Exp))
    dve(lambda e: e.tensor_sub(c_lam[:, 1:2], c_lam[:, 4:5], c_lam[:, 5:6]),
        [("ACT", a_l2)])
    P.drain("vector")
    v_lam = dve(lambda e: e.tensor_scalar(c_lam[:, 0:1], c_lam[:, 1:2],
                                          LAMBDA_INIT, None, Alu.add))

    wring_n = [0, 0]
    for h in range(min(2, NH)):
        sl = h % 2
        din(lambda e, h=h, sl=sl: e.dma_start(s_wq[sl], d_wqkv[h]),
            sem="W%d" % sl)
        wring_n[sl] += 16

    evac_bank = {0: 0, 1: 0}
    gbuf_free = {0: ([], 0), 1: ([], 0)}
    spq_free = {0: 0, 1: 0}
    tbank_free = {0: 0, 1: 0}
    diag_bank_free = 0
    pendA = []
    pendB = []
    pe_head_end_p1 = {}
    GRB = GR * 128

    def stageA(h, grp, rope_cnt, gb_cnt):
        nonlocal_state = None
        gsl = (h * NGR + grp) % 2
        qb, kb = 2 + gsl, 4 + gsl
        w = [("DVE", rope_cnt), ("ACT", tbank_free[gsl])]
        for tl in range(GR):
            def tr(e, tl=tl, gsl=gsl, qb=qb):
                return e.transpose(
                    bank(qb)[:, tl * 128 : tl * 128 + 128],
                    s_gb[gsl][:, tl * 128 : tl * 128 + 128], c_ident)
            pe(tr, w if tl == 0 else (), inc=False)
        for tl in range(GR):
            def tr(e, tl=tl, gsl=gsl, kb=kb):
                return e.transpose(
                    bank(kb)[:, tl * 128 : tl * 128 + 128],
                    s_gb[gsl][:, GRB + tl * 128 : GRB + tl * 128 + 128], c_ident)
            pe(tr, (), inc=(tl == GR - 1))
        tr_cnt = P.cnt["PE"]
        ew = [("PE", tr_cnt), ("SQ%d" % gsl, spq_free[gsl])]
        act(lambda e, gsl=gsl, qb=qb: e.copy(
            s_spq[gsl][:, 0:GRB], bank(qb)[:, 0:GRB]), ew)
        se = act(lambda e, gsl=gsl, kb=kb: e.copy(
            s_spq[gsl][:, GRB : 2 * GRB], bank(kb)[:, 0:GRB]))
        tbank_free[gsl] = se
        c0 = grp * GRB
        dout(lambda e, h=h, gsl=gsl, c0=c0: e.dma_start(
            d_qT[h][0:64, c0 : c0 + GRB], s_spq[gsl][0:64, 0:GRB]),
            [("ACT", se)], sem="SQ%d" % gsl)
        dout(lambda e, h=h, gsl=gsl, c0=c0: e.dma_start(
            d_qT[h][0:64, T + c0 : T + c0 + GRB], s_spq[gsl][64:128, 0:GRB]),
            sem="SQ%d" % gsl)
        dout(lambda e, h=h, gsl=gsl, c0=c0: e.dma_start(
            d_kT[h][0:64, c0 : c0 + GRB], s_spq[gsl][0:64, GRB : 2 * GRB]),
            sem="SQ%d" % gsl)
        sq_val = dout(lambda e, h=h, gsl=gsl, c0=c0: e.dma_start(
            d_kT[h][0:64, T + c0 : T + c0 + GRB],
            s_spq[gsl][64:128, GRB : 2 * GRB]), sem="SQ%d" % gsl)
        spq_free[gsl] = sq_val
        sv_val = dout(lambda e, h=h, gsl=gsl, c0=c0: e.dma_start(
            d_v[h][:, c0 : c0 + GRB], s_gb[gsl][:, 2 * GRB : 3 * GRB]),
            [("ACT", gb_cnt)], sem="SV%d" % gsl)
        P.drain("vector")
        pr = dve(lambda e, gsl=gsl: e.tensor_mul(
            s_gb[gsl][:, 0:GRB], s_spq[gsl][:, 0:GRB],
            s_spq[gsl][:, GRB : 2 * GRB]), [("ACT", se)])
        return (h, grp, gsl, pr, tr_cnt, sv_val)

    def stageB(h, grp, gsl, pr, tr_cnt, sv_val):
        nonlocal diag_bank_free
        c0 = grp * GRB
        dg = pe(lambda e, gsl=gsl: e.matmul(
            pb[0:2, 6 * BANK : 6 * BANK + GRB], c_hsel, s_gb[gsl][:, 0:GRB],
            start=True, stop=True),
            [("DVE", pr), ("DVE", diag_bank_free)])
        P.drain("vector")
        mn = dve(lambda e, gsl=gsl: e.tensor_scalar(
            s_gb[gsl][0:2, GRB : GRB + GRB], pb[0:2, 6 * BANK : 6 * BANK + GRB],
            -1.0, -2.0, Alu.mult, Alu.min), [("PE", dg)])
        diag_bank_free = mn
        dout(lambda e, h=h, gsl=gsl, c0=c0: e.dma_start(
            d_qT[h][64:65, c0 : c0 + GRB], s_gb[gsl][0:1, GRB : GRB + GRB]),
            [("DVE", mn)], sem="SV%d" % gsl)
        sv2 = dout(lambda e, h=h, gsl=gsl, c0=c0: e.dma_start(
            d_qT[h][64:65, T + c0 : T + c0 + GRB],
            s_gb[gsl][1:2, GRB : GRB + GRB]), sem="SV%d" % gsl)
        gbuf_free[gsl] = ([("PE", dg)], sv2)

    for h in range(NH):
        wsl = h % 2
        wsem = "W%d" % wsl
        for grp in range(NGR):
            gsl = (h * NGR + grp) % 2
            if pendB:
                stageB(*pendB.pop(0))
            for tl in range(GR):
                tt = grp * GR + tl
                psl = (h * NT + tt) % 2
                mmw = [("IN", ld_all), (wsem, wring_n[wsl]),
                       ("ACT", evac_bank[psl])]
                for et in range(NE):
                    def mm(e, tt=tt, et=et, psl=psl, wsl=wsl):
                        lhsT = s_xT[:, et * T + tt * 128 : et * T + tt * 128 + 128]
                        rhs = s_wq[wsl].rearrange("p (s x) -> p s x", s=3)[
                            :, :, et * 128 : (et + 1) * 128]
                        return e.matmul(bank(psl)[:, 0:384].rearrange(
                            "p (s x) -> p s x", s=3), lhsT, rhs,
                            start=(et == 0), stop=(et == NE - 1))
                    pe(mm, mmw if et == 0 else (), inc=(et == NE - 1))
                mm_cnt = P.cnt["PE"]
                ew = [("PE", mm_cnt), ("DVE", v_lam)]
                if tl == 0:
                    ew = ew + gbuf_free[gsl][0]
                    if gbuf_free[gsl][1]:
                        ew = ew + [("SV%d" % gsl, gbuf_free[gsl][1])]
                def evac(e, tl=tl, gsl=gsl, psl=psl):
                    dst = s_gb[gsl].rearrange("p (s x) -> p s x", s=3)[
                        :, :, tl * 128 : (tl + 1) * 128]
                    return e.copy(dst, bank(psl)[:, 0:384].rearrange("p (s x) -> p s x", s=3))
                evac_bank[psl] = act(evac, ew)
            gb_cnt = P.cnt["ACT"]

            rope_w = [("ACT", gb_cnt), ("DVE", v_lam)]
            cosg = s_cos[:, grp * GR * 64 : (grp + 1) * GR * 64].rearrange(
                "p (g k) -> p g k", g=GR)
            sing = s_sin[:, grp * GR * 64 : (grp + 1) * GR * 64].rearrange(
                "p (g k) -> p g k", g=GR)
            P.drain("vector")
            for seg in range(2):
                segap = s_gb[gsl][:, seg * GRB : (seg + 1) * GRB]
                sg = segap.rearrange("p (g b) -> p g b", g=GR)
                evn, odd = sg[:, :, 0:128:2], sg[:, :, 1:128:2]
                tA = s_tmp[0].rearrange("p (g k) -> p g k", g=GR)
                tB = s_tmp[1].rearrange("p (g k) -> p g k", g=GR)
                dve(lambda e, odd=odd, tA=tA, sing=sing: e.tensor_mul(tA, odd, sing),
                    rope_w if seg == 0 else ())
                dve(lambda e, evn=evn, tB=tB, sing=sing: e.tensor_mul(tB, evn, sing))
                P.drain("vector")
                dve(lambda e, evn=evn, cosg=cosg: e.tensor_mul(evn, evn, cosg))
                dve(lambda e, odd=odd, cosg=cosg: e.tensor_mul(odd, odd, cosg))
                P.drain("vector")
                dve(lambda e, evn=evn, tA=tA: e.tensor_sub(evn, evn, tA))
                dve(lambda e, odd=odd, tB=tB: e.tensor_add(odd, odd, tB))
                P.drain("vector")
            rope_cnt = P.cnt["DVE"]

            pendA.append((h, grp, rope_cnt, gb_cnt))
            if len(pendA) > 1:
                pendB.append(stageA(*pendA.pop(0)))
        pe_head_end_p1[h] = P.cnt["PE"]
        if h + 2 < NH:
            sl2 = (h + 2) % 2
            din(lambda e, h2=h + 2, sl2=sl2: e.dma_start(s_wq[sl2], d_wqkv[h2]),
                [("PE", pe_head_end_p1[h])], sem="W%d" % sl2)
            wring_n[sl2] += 16
    while pendA:
        pendB.append(stageA(*pendA.pop(0)))
        while len(pendB) > 1:
            stageB(*pendB.pop(0))
    while pendB:
        stageB(*pendB.pop(0))

    bar1 = {s: P.cnt[s] for s in ("PE", "ACT", "DVE", "SQ0", "SQ1", "SV0", "SV1")}
    b1w = [(k, v) for k, v in bar1.items()]

    for sl in range(2):
        dve(lambda e, sl=sl: e.memset(
            s_va[sl].rearrange("p (t c) -> p t c", t=NT)[:, :, 128:129], 1.0),
            b1w)
        dve(lambda e, sl=sl: e.memset(s_qk[sl][64:66, 2 * T : 4 * T], 1.0))
    dve(lambda e: e.memset(c_one, 1.0))
    v_ones = P.cnt["DVE"]

    q_n = [0, 0]
    pe_head_end = {}
    sa_free = {0: 0, 1: 0}
    sa_val = {}
    at_free_pe = 0
    tpack_free = {}
    slist = []
    clist = []
    sd_prev = 0
    devac_prev = 0
    cast_prev_pe = 0
    zmm_pe = {0: 0, 1: 0}
    dbg_gate = [0]

    def p2_load(h):
        sl = h % 2
        w = list(b1w)
        if h >= 2:
            w.append(("PE", pe_head_end[h - 2]))
        din(lambda e, h=h, sl=sl: e.dma_start(s_qk[sl][0:65, 0 : 2 * T], d_qT[h]),
            w + [("DVE", v_ones), ("ST", dbg_gate[0])], sem="Q%d" % sl)
        din(lambda e, h=h, sl=sl: e.dma_start(s_qk[sl][0:64, 2 * T : 4 * T], d_kT[h]),
            sem="Q%d" % sl)
        din(lambda e, h=h, sl=sl: e.dma_start(
            s_va[sl].rearrange("p (t c) -> p t c", t=NT)[:, :, 0:128],
            d_v[h].rearrange("p (t c) -> p t c", t=NT)),
            [("DVE", v_ones)], sem="Q%d" % sl)
        q_n[sl] += 48

    for h in range(min(2, NH)):
        p2_load(h)

    for h in range(NH):
        qsl = h % 2
        qsem = ("Q%d" % qsl, q_n[qsl])
        P.drain("vector")
        v_cast = dve(lambda e, qsl=qsl: e.tensor_copy(
            s_qkbf[0:66, :], s_qk[qsl][0:66, :]),
            [qsem, ("PE", cast_prev_pe)])
        ue_prev = 0
        at_out = {}
        sd_val = {0: 0}
        zlist = {}
        chunk_lastg = {}
        chunk_upe = {}

        def emit_scores(g, c, st):
            ssl = g % 2
            sw = [qsem]
            if g == 0 and h == 0:
                sw += b1w
            G = len(slist)
            if G >= 2:
                sw.append(("ACT", slist[G - 2]))
            if g < 2 and h > 0:
                sw.append(("ACT", tpack_free.get(2 * ssl, 0)))
                sw.append(("ACT", tpack_free.get(2 * ssl + 1, 0)))
            sw.append(("SD", sd_val[0]))
            for hf in range(2):
                def mm(e, c=c, st=st, ssl=ssl, hf=hf, qsl=qsl):
                    src = s_qk[qsl]
                    return e.matmul(
                        bank(2 * ssl + hf)[:, 0 : TQC * 128],
                        src[0:66, 2 * T + hf * T + st * 128 : 2 * T + hf * T + st * 128 + 128],
                        src[0:66, hf * T + c * TQC * 128 : hf * T + (c + 1) * TQC * 128],
                        start=True, stop=True)
                sc = pe(mm, sw if hf == 0 else (), inc=(hf == 1))
            xw = [("PE", sc), ("ST", dbg_gate[0])]
            if G >= 3:
                xw.append(("PE", clist[G - 3][0]))
                xw.append(("DVE", clist[G - 3][1]))
            sslot = G % 3
            def xp(e, ssl=ssl, sslot=sslot):
                src = pb[:, 2 * ssl * BANK : (2 * ssl + 2) * BANK].rearrange(
                    "p (b x) -> p b x", b=2)[:, :, 0 : TQC * 128]
                dst = s_E[sslot].rearrange("p (b x) -> p b x", b=2)
                return e.activation(dst, src, Act.Exp, bias=0.0, scale=SC_SCALE)
            e_cnt = act(xp, xw)
            slist.append(e_cnt)
            HW = TQC * 128
            zsl = c % 2
            P.drain("vector")
            if st == 0:
                zc = dve(lambda e, sslot=sslot, zsl=zsl: e.tensor_copy(
                    s_zacc[zsl][:, 0 : 2 * HW], s_E[sslot]),
                    [("ACT", e_cnt), ("PE", zmm_pe[zsl])])
            else:
                zc = dve(lambda e, sslot=sslot, zsl=zsl: e.tensor_add(
                    s_zacc[zsl][:, 0 : 2 * HW], s_zacc[zsl][:, 0 : 2 * HW],
                    s_E[sslot]), [("ACT", e_cnt)])
            zlist[g] = (G, e_cnt, zc)

        def emit_av(g, st):
            G2, e_cnt, zc = zlist[g]
            sslot = G2 % 3
            avw = [("ACT", e_cnt)]
            if st == 0:
                avw.append(("ACT", ue_prev))
            HW = TQC * 128
            for hf in range(2):
                def mavu(e, hf=hf, sslot=sslot, st=st, qsl=qsl, HW=HW):
                    return e.matmul(
                        pb[:, (4 + hf) * BANK : (4 + hf) * BANK + HW],
                        s_va[qsl][:, st * 129 : st * 129 + 128],
                        s_E[sslot][:, hf * HW : (hf + 1) * HW],
                        start=(st == 0), stop=(st == NT - 1))
                pe(mavu, avw if hf == 0 else (), inc=(hf == 1))
            if st == NT - 1:
                chunk_upe[len(chunk_upe)] = P.cnt["PE"]
            clist.append((P.cnt["PE"], zc))

        def pre_step(g, tt, hf, sc2):
            ssl = g % 2
            sw = [qsem, ("DVE", v_cast)]
            if g == 0 and h == 0:
                sw += b1w
            G = len(slist)
            if G >= 2:
                sw.append(("ACT", slist[G - 2]))
            if g < 2 and h > 0:
                sw.append(("ACT", tpack_free.get(2 * ssl, 0)))
                sw.append(("ACT", tpack_free.get(2 * ssl + 1, 0)))
            for hb in range(PB):
                def mm(e, tt=tt, hf=hf, sc2=sc2, hb=hb, ssl=ssl):
                    return e.matmul(
                        bank(2 * ssl + hb)[:, 0 : min(512, PW)],
                        s_qkbf[0:65, hf * T + tt * 128 : hf * T + tt * 128 + 128],
                        s_qkbf[0:65, 2 * T + hf * T + sc2 * PW + hb * 512 :
                               2 * T + hf * T + sc2 * PW + hb * 512 + min(512, PW)],
                        start=True, stop=True)
                sc = pe(mm, sw if hb == 0 else (), inc=(hb == PB - 1))
            sslot = G % 3
            xw = [("PE", sc), ("ST", dbg_gate[0])]
            if G >= 3:
                xw.append(("PE", clist[G - 3][0]))
                xw.append(("DVE", clist[G - 3][1]))
            def xp(e, tt=tt, hf=hf, sc2=sc2, ssl=ssl, sslot=sslot):
                src = pb[:, 2 * ssl * BANK : 2 * ssl * BANK + PW]
                dst = s_E[sslot][:, 0:PW]
                acc = s_zb[:, (tt * 2 + hf) * NSC2 + sc2 : (tt * 2 + hf) * NSC2 + sc2 + 1]
                return e.activation(dst, src, Act.Exp, bias=PRE_BIAS, scale=BETA,
                                    accum_out=acc)
            e_cnt = act(xp, xw)
            slist.append(e_cnt)
            clist.append((P.cnt["PE"], 0))

        def delta_head():
            nonlocal sd_prev, devac_prev
            z_act = slist[-1]
            P.drain("vector")
            v1 = dve(lambda e: e.tensor_reduce(
                s_zc, s_zb[:, 0 : 2 * NT * NSC2].rearrange("p (j k) -> p j k", k=NSC2), AxX, Alu.add),
                [("ACT", z_act)])
            a1 = act(lambda e: e.activation(s_zd, s_zc, Act.Ln, bias=1e-30),
                     [("DVE", v1)])
            P.drain("vector")
            def aff(e):
                src = bass.AP(s_zd.tensor, s_zd.offset,
                              [s_zd.ap[0], [1, 2], [2, NT]])
                dst = s_dt[:, 0 : 2 * NT]
                return e.tensor_scalar(dst, src, -1.0 / BETA,
                                       PRE_BIAS / BETA + MARGIN_RAW,
                                       Alu.mult, Alu.add)
            v2 = dve(aff, [("ACT", a1)])
            tr = pe(lambda e: e.transpose(
                pb[0 : 2 * NT, 7 * BANK : 7 * BANK + 128], s_dt[:, 0 : 2 * NT],
                c_ident), [("DVE", v2), ("ACT", devac_prev)])
            dv = act(lambda e: e.copy(s_dstage[0 : 2 * NT, 0:128],
                                      pb[0 : 2 * NT, 7 * BANK : 7 * BANK + 128]),
                     [("PE", tr), ("SD", sd_prev)])
            devac_prev = dv
            def dmad0(e, qsl=qsl):
                return e.dma_start(s_qk[qsl][65:66, 0:T], s_dstage[0:NT, 0:128])
            dout(dmad0, [("ACT", dv)], sem="SD")
            def dmad1(e, qsl=qsl):
                return e.dma_start(s_qk[qsl][65:66, T : 2 * T],
                                   s_dstage[NT : 2 * NT, 0:128])
            sd_prev = dout(dmad1, sem="SD")
            sd_val[0] = sd_prev

        tail_state = {}

        def tail_part1(c):
            HW = TQC * 128
            zw_mm = [("DVE", zlist[chunk_lastg[c]][2]), ("ACT", devac_prev)]
            for hf in range(2):
                def zmm(e, hf=hf, HW=HW, zsl=c % 2):
                    return e.matmul(
                        pb[0:1, (6 + hf) * BANK : (6 + hf) * BANK + HW],
                        c_one, s_zacc[zsl][:, hf * HW : (hf + 1) * HW],
                        start=True, stop=True)
                pe(zmm, zw_mm if hf == 0 else (), inc=(hf == 1))
            zmm_cnt = P.cnt["PE"]
            zmm_pe[c % 2] = zmm_cnt
            def zev(e, HW=HW):
                pap = pb[:]
                src = bass.AP(pap.tensor, pap.offset + 6 * BANK,
                              [[pap.ap[0][0], 1], [BANK, 2], [1, HW]])
                return e.copy(s_zw[0:1, 0 : 2 * HW].rearrange(
                    "p (b x) -> p b x", b=2), src)
            zev_cnt = act(zev, [("PE", zmm_cnt)])
            ztree = zev_cnt
            ut_w = [("PE", chunk_upe[c]), ("ST", dbg_gate[0])]
            for hf in range(2):
                def utev(e, hf=hf, HW=HW):
                    return e.copy(s_uT[:, hf * HW : (hf + 1) * HW],
                                  pb[:, (4 + hf) * BANK : (4 + hf) * BANK + HW])
                act(utev, ut_w if hf == 0 else ())
            utev_cnt = P.cnt["ACT"]
            tail_state[c] = (utev_cnt, ztree)

        def tail_part2(c):
            nonlocal ue_prev
            HW = TQC * 128
            utev_cnt, ztree = tail_state[c]
            tw = [("ACT", utev_cnt), ("ACT", ztree)]
            for a in range(2 * TQC):
                ub, us = 4 + a // 3, a % 3
                def trU(e, a=a, ub=ub, us=us):
                    return e.transpose(
                        pb[:, ub * BANK + us * 129 : ub * BANK + us * 129 + 128],
                        s_uT[:, a * 128 : (a + 1) * 128], c_ident)
                pe(trU, tw if a == 0 else ())
                def trZ(e, a=a, ub=ub, us=us):
                    return e.transpose(
                        pb[:, ub * BANK + us * 129 + 128 : ub * BANK + us * 129 + 129],
                        s_zw[0:1, a * 128 : (a + 1) * 128], c_ident[0:1, 0:1])
                pe(trZ, (), inc=(a == 2 * TQC - 1))
            tr_cnt2 = P.cnt["PE"]
            def uev(e):
                pap = pb[:]
                src = bass.AP(pap.tensor, pap.offset + 4 * BANK,
                              [pap.ap[0], [BANK, 2], [129, 3], [1, 129]])
                dst = s_ub[:, 0 : 6 * 129].rearrange("p (b s x) -> p b s x", b=2, s=3)
                return e.copy(dst, src)
            act(uev, [("PE", tr_cnt2), ("ST", dbg_gate[0])])
            def uev2(e):
                pap = pb[:]
                src = bass.AP(pap.tensor, pap.offset + 6 * BANK,
                              [pap.ap[0], [129, 2], [1, 129]])
                dst = s_ub[:, 6 * 129 : 8 * 129].rearrange("p (s x) -> p s x", s=2)
                return e.copy(dst, src)
            ue_prev = act(uev2)
            asw = [("ACT", ue_prev), ("DVE", v_lam), ("ST", dbg_gate[0])]
            if h > 0 and c == 0:
                asw.append(("PE", at_free_pe))
            P.drain("scalar")
            for sub in range(TQC):
                a0, a1 = sub, TQC + sub
                act(lambda e, sub=sub, a0=a0: e.activation(
                    s_st[:, 24 + 2 * sub : 25 + 2 * sub],
                    s_ub[:, a0 * 129 + 128 : a0 * 129 + 129], Act.Ln),
                    asw if sub == 0 else ())
                act(lambda e, sub=sub, a1=a1: e.activation(
                    s_st[:, 25 + 2 * sub : 26 + 2 * sub],
                    s_ub[:, a1 * 129 + 128 : a1 * 129 + 129], Act.Ln))
            P.drain("scalar")
            for sub in range(TQC):
                act(lambda e, sub=sub: e.activation(
                    s_st[:, 2 * sub : 2 * sub + 1],
                    s_st[:, 24 + 2 * sub : 25 + 2 * sub], Act.Exp, scale=-1.0))
                act(lambda e, sub=sub: e.activation(
                    s_st[:, 2 * sub + 1 : 2 * sub + 2],
                    s_st[:, 25 + 2 * sub : 26 + 2 * sub], Act.Exp, scale=-1.0))
            a_rz = P.cnt["ACT"]
            P.drain("vector")
            for sub in range(TQC):
                if sub == 0:
                    pass
                a1 = TQC + sub
                dve(lambda e, sub=sub, a1=a1: e.tensor_scalar(
                    s_as[:, sub * 128 : (sub + 1) * 128],
                    s_ub[:, a1 * 129 : a1 * 129 + 128],
                    s_st[:, 2 * sub + 1 : 2 * sub + 2], c_lam[:, 0:1],
                    Alu.mult, Alu.mult), [("ACT", a_rz)] if sub == 0 else ())
            P.drain("vector")
            for sub in range(TQC):
                a0 = sub
                dve(lambda e, sub=sub, a0=a0: e.scalar_tensor_tensor(
                    s_as2[:, sub * 128 : (sub + 1) * 128],
                    s_ub[:, a0 * 129 : a0 * 129 + 128],
                    s_st[:, 2 * sub : 2 * sub + 1],
                    s_as[:, sub * 128 : (sub + 1) * 128],
                    Alu.mult, Alu.subtract))
            P.drain("vector")
            for sub in range(TQC):
                dve(lambda e, sub=sub: e.tensor_mul(
                    s_as[:, sub * 128 : (sub + 1) * 128],
                    s_as2[:, sub * 128 : (sub + 1) * 128],
                    s_as2[:, sub * 128 : (sub + 1) * 128]))
            P.drain("vector")
            for sub in range(TQC):
                dve(lambda e, sub=sub: e.reduce_sum(
                    s_st[:, 8 + sub : 9 + sub],
                    s_as[:, sub * 128 : (sub + 1) * 128], AxX))
            v_red = P.cnt["DVE"]
            for sub in range(TQC):
                act(lambda e, sub=sub: e.activation(
                    s_st[:, 12 + sub : 13 + sub], s_st[:, 8 + sub : 9 + sub],
                    Act.Ln, bias=EPS, scale=1.0 / 128.0),
                    [("DVE", v_red)] if sub == 0 else ())
            P.drain("scalar")
            for sub in range(TQC):
                act(lambda e, sub=sub: e.activation(
                    s_st[:, 16 + sub : 17 + sub], s_st[:, 12 + sub : 13 + sub],
                    Act.Exp, scale=-0.5))
            a_rs = P.cnt["ACT"]
            for sub in range(TQC):
                dve(lambda e, sub=sub, c=c: e.scalar_tensor_tensor(
                    s_at[:, (c * TQC + sub) * 128 : (c * TQC + sub + 1) * 128],
                    s_as2[:, sub * 128 : (sub + 1) * 128],
                    s_st[:, 16 + sub : 17 + sub], c_subln,
                    Alu.mult, Alu.mult), [("ACT", a_rs)] if sub == 0 else ())
            at_out[c] = P.cnt["DVE"]
            if debug_scratch and h == 0 and c == NCH - 1:
                dout(lambda e, qsl=qsl: e.dma_start(d_dbg2[0][:, 0 : 4 * T], s_qk[qsl]),
                     [("SD", sd_val[0]), ("DVE", at_out[c])], sem="ST")
                dout(lambda e: e.dma_start(d_dbg2[1][:, 0 : 9 * 129], s_ub), sem="ST")
                dout(lambda e: e.dma_start(d_dbg2[2][:, 0 : NT * 128], s_at), sem="ST")
                dout(lambda e: e.dma_start(d_dbg2[3][:, 0 : 2 * TQC * 128], s_E[1]), sem="ST")
                dout(lambda e: e.dma_start(d_dbg2[4][:, 0:64], s_st), sem="ST")
                dout(lambda e: e.dma_start(d_dbg2[5][:, 0 : TQC * 128], s_as), sem="ST")
                dout(lambda e: e.dma_start(d_dbg2[6][:, 0 : TQC * 128], s_as2), sem="ST")
                dout(lambda e: e.dma_start(d_dbg2[7][:, 0:8], c_lam), sem="ST")
                dout(lambda e: e.dma_start(d_dbg2[7][:, 8:136], c_subln), sem="ST")
                dbg_gate[0] = P.cnt["ST"]

        g = 0
        if use_lse:
            for tt in range(NT):
                for hf in range(2):
                    for sc2 in range(NSC2):
                        pre_step(g, tt, hf, sc2)
                        g += 1
            delta_head()
        else:
            sd_val[0] = 0
        uq = []
        for c in range(NCH):
            for st in range(NT):
                if c > 0 and st == 3:
                    emit_av(*uq.pop(0))
                    emit_av(*uq.pop(0))
                emit_scores(g, c, st)
                uq.append((g, st))
                if st == NT - 1:
                    chunk_lastg[c] = g
                g += 1
                if c == 0:
                    if st >= 2:
                        emit_av(*uq.pop(0))
                elif st == 0:
                    emit_av(*uq.pop(0))
                elif st == 1:
                    emit_av(*uq.pop(0))
                    tail_part1(c - 1)
                elif st == 2:
                    tail_part2(c - 1)
                elif st >= 4:
                    emit_av(*uq.pop(0))
        emit_av(*uq.pop(0))
        emit_av(*uq.pop(0))
        tail_part1(NCH - 1)
        tail_part2(NCH - 1)
        cast_prev_pe = clist[-1][0]

        asl = h % 2
        last_exp = slist[-1]
        for pk in range(NT // 4):
            pkb = pk % 4
            cdone = min(NCH - 1, (4 * pk + 3) // TQC)
            tw = [("DVE", at_out[cdone]), ("ACT", last_exp),
                  ("ACT", tpack_free.get(pkb, 0))]
            for tl in range(4):
                tt = pk * 4 + tl
                pe(lambda e, tt=tt, tl=tl, pkb=pkb: e.transpose(
                    bank(pkb)[:, tl * 128 : (tl + 1) * 128],
                    s_at[:, tt * 128 : (tt + 1) * 128], c_ident),
                   tw if tl == 0 else (), inc=(tl == 3))
            tr = P.cnt["PE"]
            ew = [("PE", tr)]
            if pk == 0:
                ew.append(("SA%d" % asl, sa_free[asl]))
            tpack_free[pkb] = act(lambda e, pk=pk, asl=asl, pkb=pkb: e.copy(
                s_aT[asl][:, pk * 512 : pk * 512 + 512], bank(pkb)[:, 0:512]), ew)
        pe_head_end[h] = P.cnt["PE"]
        a_packs = P.cnt["ACT"]
        at_free_pe = pe_head_end[h]
        sa_free[asl] = dout(lambda e, h=h, asl=asl: e.dma_start(
            d_aT[h], s_aT[asl]), [("ACT", a_packs)], sem="SA%d" % asl)
        sa_val[h] = sa_free[asl]
        if h + 2 < NH:
            p2_load(h + 2)

    bar2 = {s: P.cnt[s] for s in ("PE", "ACT", "DVE", "SA0", "SA1")}
    b2w = [(k, v) for k, v in bar2.items()]

    gn_at = {}

    def g_loads(g, w):
        gp = g % 2
        et = g
        din(lambda e, et=et, gp=gp: e.dma_start(s_ga[gp], d_aT[et]),
            w, sem="G%d0" % gp)
        din(lambda e, et=et, gp=gp: e.dma_start(s_gw[gp], d_wout[et]),
            sem="G%d0" % gp)
        gn_at[g] = ("G%d0" % gp, P.cnt["G%d0" % gp])

    early_ok = s_ga[0].offset * 4 >= p2.off
    g_loads(0, [("SA0", sa_val[0])] if early_ok else list(b2w))
    g_loads(1, [("SA1", sa_val[1])] if early_ok else list(b2w))
    NYG = max(1, NT // 4)
    for tt in range(NT):
        din(lambda e, tt=tt: e.dma_start(s_y[:, tt * E : (tt + 1) * E], d_x[tt]),
            b2w, sem="Y%d" % (tt * 4 // NT))
    yin_done = [("Y%d" % i, P.cnt["Y%d" % i]) for i in range(4)]

    w2q = 0
    for kt in range(NE):
        din(lambda e, kt=kt: e.dma_start(s_w2b, d_w2[kt]),
            list(b2w) + ([("DVE", w2q)] if kt else []), sem="W2S")
        w2q = dve(lambda e, kt=kt: e.tensor_scalar(
            s_d2[:, kt * E : (kt + 1) * E], s_w2b, c_scl[:, 1:2], None, Alu.mult),
            [("W2S", 16 * (kt + 1))])

    wout_pe_grp = {}
    dve_add = {}
    ln1 = {}

    def emit_ln1(tt):
        yt = s_y[:, tt * E : (tt + 1) * E]
        EH2 = E // 2
        w0 = [("DVE", dve_add[(NWG - 1, tt)]), ("DVE", w2q)]
        P.drain("vector")
        dve(lambda e, yt=yt: e.tensor_mul(s_sq1, yt[:, 0:EH2], yt[:, 0:EH2]), w0)
        dve(lambda e, yt=yt: e.reduce_sum(s_st1[:, 0:1], yt, AxX))
        P.drain("vector")
        dve(lambda e: e.reduce_sum(s_st1[:, 8:9], s_sq1, AxX))
        P.drain("vector")
        dve(lambda e, yt=yt: e.tensor_mul(s_sq1, yt[:, EH2:E], yt[:, EH2:E]))
        P.drain("vector")
        dve(lambda e: e.reduce_sum(s_st1[:, 9:10], s_sq1, AxX))
        dve(lambda e: e.tensor_scalar(s_st1[:, 2:3], s_st1[:, 0:1],
                                      1.0 / E, None, Alu.mult))
        P.drain("vector")
        dve(lambda e: e.tensor_add(s_st1[:, 1:2], s_st1[:, 8:9], s_st1[:, 9:10]))
        dve(lambda e: e.tensor_mul(s_st1[:, 3:4], s_st1[:, 2:3], s_st1[:, 2:3]))
        P.drain("vector")
        dve(lambda e: e.tensor_scalar(s_st1[:, 4:5], s_st1[:, 1:2],
                                      1.0 / E, None, Alu.mult))
        P.drain("vector")
        v_var = dve(lambda e: e.tensor_sub(s_st1[:, 5:6], s_st1[:, 4:5],
                                           s_st1[:, 3:4]))
        act(lambda e: e.activation(s_st1[:, 6:7], s_st1[:, 5:6], Act.Ln,
                                   bias=EPS), [("DVE", v_var)])
        P.drain("scalar")
        a_rs = act(lambda e: e.activation(s_st1[:, 7:8], s_st1[:, 6:7],
                                          Act.Exp, scale=-0.5))
        ln1[tt] = dve(lambda e, yt=yt: e.tensor_scalar(
            yt, yt, s_st1[:, 2:3], s_st1[:, 7:8], Alu.subtract, Alu.mult),
            [("ACT", a_rs)])
    for g in range(NWG):
        for tt in range(NT):
            tb = 4 * (tt % 2)
            mw = [gn_at[g],
                  ("DVE", dve_add.get((g, tt - 2), dve_add.get((g - 1, NT - 2 + tt), 0)))]
            for ch in range(ECH):
                def mwo(e, ch=ch, tt=tt, tb=tb, gp=g % 2):
                    return e.matmul(bank(tb + ch),
                             s_ga[gp][:, tt * 128 : (tt + 1) * 128],
                             s_gw[gp][:, ch * 512 : (ch + 1) * 512],
                             start=True, stop=True)
                pe(mwo, mw if ch == 0 else (), inc=(ch == ECH - 1))
            g_pe = P.cnt["PE"]
            aw = [("PE", g_pe)]
            if g == 0:
                aw.append(yin_done[tt * 4 // NT])
            dve_add[(g, tt)] = dve(lambda e, tt=tt, tb=tb: e.tensor_add(
                s_y[:, tt * E : (tt + 1) * E], s_y[:, tt * E : (tt + 1) * E],
                pb[:, tb * BANK : (tb + ECH) * BANK]), aw)
            if g == NWG - 1:
                emit_ln1(tt)
        wout_pe_grp[g] = P.cnt["PE"]
        if g + 2 < NWG:
            g_loads(g + 2, [("PE", wout_pe_grp[g])])

    wout_all_pe = P.cnt["PE"]
    g_done = [("G00", P.cnt["G00"]), ("G10", P.cnt["G10"]), ("W2S", P.cnt["W2S"]),
              ("DVE", w2q), ("PE", wout_all_pe)]

    dve_r2 = {}
    st_n = 0
    for half in range(NHALF):
        tts = range(half * HTT, (half + 1) * HTT)
        dq_prev = {0: 0, 1: 0}
        r2_prev = max(dve_r2.values()) if dve_r2 else 0
        for tt in tts:
            tl = tt - half * HTT
            for pk in range(NE // 4):
                pkb = pk % 2
                tw = [("DVE", ln1[tt]), ("ACT", dq_prev[pkb]),
                      ("PE", wout_all_pe), ("DVE", r2_prev),
                      ("DVE", dve_add[(NWG - 1, NT - 1)])]
                for q4 in range(4):
                    et2 = pk * 4 + q4
                    pe(lambda e, tt=tt, et2=et2, q4=q4, pkb=pkb: e.transpose(
                        bank(pkb)[:, q4 * 128 : (q4 + 1) * 128],
                        s_y[:, tt * E + et2 * 128 : tt * E + (et2 + 1) * 128],
                        c_ident), tw if q4 == 0 else (), inc=(q4 == 3))
                trc = P.cnt["PE"]
                def qev(e, pk=pk, tl=tl, pkb=pkb):
                    dst = bass.AP(s_d1.tensor, s_d1.offset + (pk * 4 * HTT + tl) * 128,
                                  [s_d1.ap[0], [HTT * 128, 4], [1, 128]])
                    return e.activation(dst, bank(pkb)[:, 0:512], Act.Copy,
                                 scale=c_scl[:, 0:1], bias=0.0)
                dq_prev[pkb] = act(qev, [("PE", trc)])
        d1_done = P.cnt["ACT"]
        for tt in tts:
            tl = tt - half * HTT
            tb = 4 * (tl % 2)
            fw = [("ACT", d1_done), ("DVE", dve_r2.get(tt - 2, r2_prev))]
            for ch in range(ECH):
                for kt in range(NE):
                    def mf(e, tl=tl, ch=ch, kt=kt, tb=tb):
                        return e.matmul(bank(tb + ch),
                                 s_d1[:, kt * HTT * 128 + tl * 128 : kt * HTT * 128 + (tl + 1) * 128],
                                 s_d2[:, kt * E + ch * 512 : kt * E + (ch + 1) * 512],
                                 start=(kt == 0), stop=(kt == NE - 1))
                    pe(mf, fw if (ch == 0 and kt == 0) else (),
                       inc=(ch == ECH - 1 and kt == NE - 1))
            f_pe = P.cnt["PE"]
            yt = s_y[:, tt * E : (tt + 1) * E]
            P.drain("vector")
            dve(lambda e, yt=yt, tb=tb: e.scalar_tensor_tensor(
                yt, pb[:, tb * BANK : (tb + ECH) * BANK], c_lam[:, 6:7], yt,
                Alu.mult, Alu.add), [("PE", f_pe), ("DVE", v_rdq)])
            P.drain("vector")
            dve(lambda e, yt=yt: e.tensor_mul(s_sq, yt, yt))
            dve(lambda e, yt=yt: e.reduce_sum(s_st3[:, 0:1], yt, AxX))
            P.drain("vector")
            dve(lambda e: e.reduce_sum(s_st3[:, 1:2], s_sq, AxX))
            dve(lambda e: e.tensor_scalar(s_st3[:, 2:3], s_st3[:, 0:1],
                                          1.0 / E, None, Alu.mult))
            P.drain("vector")
            dve(lambda e: e.tensor_mul(s_st3[:, 3:4], s_st3[:, 2:3], s_st3[:, 2:3]))
            dve(lambda e: e.tensor_scalar(s_st3[:, 4:5], s_st3[:, 1:2],
                                          1.0 / E, None, Alu.mult))
            P.drain("vector")
            v_var = dve(lambda e: e.tensor_sub(s_st3[:, 5:6], s_st3[:, 4:5],
                                               s_st3[:, 3:4]))
            act(lambda e: e.activation(s_st3[:, 6:7], s_st3[:, 5:6], Act.Ln,
                                       bias=EPS), [("DVE", v_var)])
            P.drain("scalar")
            a_rs = act(lambda e: e.activation(s_st3[:, 7:8], s_st3[:, 6:7],
                                              Act.Exp, scale=-0.5))
            dve_r2[tt] = dve(lambda e, yt=yt: e.tensor_scalar(
                yt, yt, s_st3[:, 2:3], s_st3[:, 7:8], Alu.subtract, Alu.mult),
                [("ACT", a_rs)])
            dout(lambda e, tt=tt: e.dma_start(d_out[tt], s_y[:, tt * E : (tt + 1) * E]),
                 [("DVE", dve_r2[tt])])
            st_n += 16
    if debug_scratch:
        d_dbg = nc.dram_tensor("dbg", [8, 128, 4 * T], F32, kind="ExternalOutput").ap()
        enda = [("DVE", P.cnt["DVE"]), ("ACT", P.cnt["ACT"]), ("PE", P.cnt["PE"])]
        dout(lambda e: e.dma_start(d_dbg[0][:, 0 : NT * 128], s_at), enda, sem="ST")
        st_n += 16
        dout(lambda e: e.dma_start(d_dbg[1][:, 0 : 9 * 129], s_ub), sem="ST")
        st_n += 16
        dout(lambda e: e.dma_start(d_dbg[2], s_qk[0]), sem="ST")
        st_n += 16
        dout(lambda e: e.dma_start(d_dbg[3], s_qk[1]), sem="ST")
        st_n += 16
        dout(lambda e: e.dma_start(d_dbg[4][:, 0 : 2 * TQC * 128], s_E[0]), sem="ST")
        st_n += 16
        dout(lambda e: e.dma_start(d_dbg[5][:, 0 : 2 * TQC * 128], s_E[1]), sem="ST")
        st_n += 16
        dout(lambda e: e.dma_start(d_dbg[6][:, 0 : NT * 129], s_va[1]), sem="ST")
        st_n += 16
    P.emit("gpsimd", None, [("ST", P.cnt["ST"])])

    semh = {}
    import contextlib
    stack = contextlib.ExitStack()
    nc._semstack = stack
    for s in P.cnt:
        semh[s] = stack.enter_context(nc.semaphore(s))

    with nc.Block() as blk:
        def runner(eng_obj, name):
            for fn, waits, inc, inc_n in P.ops[name]:
                for s, v in waits:
                    eng_obj.wait_ge(semh[s], v)
                if fn == "drain":
                    eng_obj.drain()
                    continue
                if fn is None:
                    continue
                inst = fn(eng_obj)
                if inc is not None and inst is not None:
                    inst.then_inc(semh[inc], inc_n)

        blk.sync(lambda e: runner(e, "sync"))
        blk.gpsimd(lambda e: runner(e, "gpsimd"))
        blk.tensor(lambda e: runner(e, "tensor"))
        blk.scalar(lambda e: runner(e, "scalar"))
        blk.vector(lambda e: runner(e, "vector"))

    return nc



def stage_shared(wq, wk, wv, wout, w2, lq1, lq2, lk1, lk2, subln, scale, scale1,
                 NT=16, NH=16):
    NE = NH
    T, E = 128 * NT, 128 * NE
    f32 = np.float32

    def whead(w):
        r = w.reshape(NH, 128, NE, 128)
        return np.ascontiguousarray(r.transpose(0, 3, 2, 1)).reshape(NH, 128, E)

    wqkv = np.ascontiguousarray(
        np.concatenate([whead(wq), whead(wk), whead(wv)], axis=2))
    wout_img = np.ascontiguousarray(wout.T.reshape(NE, 128, E))
    w2_img = np.ascontiguousarray(w2.reshape(NE, 128, E))

    inv = 1.0 / 10000.0 ** (np.arange(0, 64, 2)[:32].astype(np.float64) / 64)
    ang = np.outer(np.arange(T), inv)
    cos = np.cos(ang).astype(f32)
    sin = np.sin(ang).astype(f32)
    cos_img = np.ascontiguousarray(
        np.tile(cos, (1, 2)).reshape(NT, 128, 64).transpose(1, 0, 2).reshape(128, NT * 64))
    sin_img = np.ascontiguousarray(
        np.tile(sin, (1, 2)).reshape(NT, 128, 64).transpose(1, 0, 2).reshape(128, NT * 64))

    ident = np.eye(128, dtype=f32)
    subln_img = np.ascontiguousarray(np.broadcast_to(subln[None, :], (128, 128)))
    lam_img = np.ascontiguousarray(np.broadcast_to(
        np.concatenate([lq1, lk1, lq2, lk2])[None, :], (128, 256)))
    scl_img = np.ascontiguousarray(np.broadcast_to(
        np.stack([np.float32(scale), np.float32(scale1)])[None, :], (128, 2)))
    hsel = np.zeros((128, 2), dtype=f32)
    hsel[0:64, 0] = 1.0
    hsel[64:128, 1] = 1.0
    return dict(wqkv=wqkv, wout=wout_img, w2=w2_img, cos=cos_img, sin=sin_img,
                ident=ident, subln=subln_img, lam=lam_img, scl=scl_img,
                hsel=hsel)


def stage_x(xb, NT=16, NH=16):
    NE = NH
    T, E = 128 * NT, 128 * NE
    xT = np.ascontiguousarray(xb.T.reshape(NE, 128, T))
    xi = np.ascontiguousarray(xb.reshape(NT, 128, E))
    return xT, xi


def kernel(L0_attn_x, L0_attn_wq, L0_attn_wk, L0_attn_wv, L0_attn_wout,
           L0_attn_lambda_q1, L0_attn_lambda_q2, L0_attn_lambda_k1,
           L0_attn_lambda_k2, L0_attn_subln_weight, L0_ffn_input2,
           L0_ffn_scale, L0_ffn_scale1):
    NT = NH = 16
    x = np.ascontiguousarray(np.asarray(L0_attn_x, np.float32))
    shared = stage_shared(
        np.asarray(L0_attn_wq, np.float32), np.asarray(L0_attn_wk, np.float32),
        np.asarray(L0_attn_wv, np.float32), np.asarray(L0_attn_wout, np.float32),
        np.asarray(L0_ffn_input2, np.float32),
        np.asarray(L0_attn_lambda_q1, np.float32),
        np.asarray(L0_attn_lambda_q2, np.float32),
        np.asarray(L0_attn_lambda_k1, np.float32),
        np.asarray(L0_attn_lambda_k2, np.float32),
        np.asarray(L0_attn_subln_weight, np.float32),
        np.asarray(L0_ffn_scale, np.float32), np.asarray(L0_ffn_scale1, np.float32),
        NT, NH)
    in_maps = []
    for b in range(8):
        xT, xi = stage_x(x[b], NT, NH)
        m = dict(shared)
        m["xT"] = xT
        m["x"] = xi
        in_maps.append(m)
    nc = build_program(NT, NH)
    res = run_bass_kernel_spmd(nc, in_maps, core_ids=list(range(8)))
    outs = [np.asarray(r["out"]).reshape(2048, 2048) for r in res.results]
    return np.stack(outs, axis=0)


# version 18 (final)
# speedup vs baseline: 1.0581x; 1.0581x over previous
import numpy as np
import concourse.bass as bass
import concourse.mybir as mybir
from concourse.bass_utils import run_bass_kernel_spmd

F32 = mybir.dt.float32
FP8 = mybir.dt.float8e4
Alu = mybir.AluOpType
Act = mybir.ActivationFunctionType
AxX = mybir.AxisListType.X

SC_SCALE = float(128.0 ** 0.5)
BETA = 2.5
PRE_BIAS = -20.0
MARGIN_RAW = 20.0 / SC_SCALE
EPS = 1e-5
LAMBDA_INIT = 0.8 - 0.6 * float(np.exp(-0.3 * 12))
BANK = 512


def _al(x, a=32):
    return (x + a - 1) // a * a


class Plan:
    ENGINES = ("sync", "gpsimd", "tensor", "scalar", "vector")

    def __init__(self):
        self.ops = {e: [] for e in self.ENGINES}
        self.cnt = {}
        self.last_wait = {e: {} for e in self.ENGINES}

    def sem(self, name):
        self.cnt.setdefault(name, 0)

    def emit(self, eng, fn, waits=(), inc=None, inc_n=1):
        w = []
        for s, v in waits:
            assert v <= self.cnt[s], ("future wait", eng, s, v, self.cnt[s])
            if self.last_wait[eng].get(s, -1) < v:
                self.last_wait[eng][s] = v
                w.append((s, v))
        self.ops[eng].append((fn, w, inc, inc_n))
        if inc is not None:
            self.cnt[inc] += inc_n
            return self.cnt[inc]
        return None

    def drain(self, eng):
        self.ops[eng].append(("drain", [], None, 0))


def build_program(NT=16, NH=16, debug_scratch=False, use_lse=True):
    NE = NH
    T, E = 128 * NT, 128 * NE
    TQC = min(4, NT)
    NCH = NT // TQC
    GR = min(4, NT)
    NGR = NT // GR
    WG = 1
    NWG = NE // WG
    NHALF = 2 if NT >= 2 else 1
    HTT = NT // NHALF
    PW = min(1024, T)
    NSC2 = T // PW
    PB = PW // 512
    ECH = max(1, E // 512)

    nc = bass.Bass("TRN2", target_bir_lowering=False)

    d_xT = nc.dram_tensor("xT", [NE, 128, T], F32, kind="ExternalInput").ap()
    d_x = nc.dram_tensor("x", [NT, 128, E], F32, kind="ExternalInput").ap()
    d_wqkv = nc.dram_tensor("wqkv", [NH, 128, 3 * E], F32, kind="ExternalInput").ap()
    d_wout = nc.dram_tensor("wout", [NE, 128, E], F32, kind="ExternalInput").ap()
    d_w2 = nc.dram_tensor("w2", [NE, 128, E], F32, kind="ExternalInput").ap()
    d_cos = nc.dram_tensor("cos", [128, NT * 64], F32, kind="ExternalInput").ap()
    d_sin = nc.dram_tensor("sin", [128, NT * 64], F32, kind="ExternalInput").ap()
    d_ident = nc.dram_tensor("ident", [128, 128], F32, kind="ExternalInput").ap()
    d_subln = nc.dram_tensor("subln", [128, 128], F32, kind="ExternalInput").ap()
    d_lam = nc.dram_tensor("lam", [128, 256], F32, kind="ExternalInput").ap()
    d_scl = nc.dram_tensor("scl", [128, 2], F32, kind="ExternalInput").ap()
    d_hsel = nc.dram_tensor("hsel", [128, 2], F32, kind="ExternalInput").ap()
    d_out = nc.dram_tensor("out", [NT, 128, E], F32, kind="ExternalOutput").ap()
    skind = "ExternalOutput" if debug_scratch else "Internal"
    d_dbg2 = (nc.dram_tensor("dbg2", [8, 128, 9 * T], F32, kind="ExternalOutput").ap()
              if debug_scratch else None)
    d_qT = nc.dram_tensor("scr_qT", [NH, 65, 2 * T], F32, kind=skind).ap()
    d_kT = nc.dram_tensor("scr_kT", [NH, 64, 2 * T], F32, kind=skind).ap()
    d_v = nc.dram_tensor("scr_v", [NH, 128, T], F32, kind=skind).ap()
    d_aT = nc.dram_tensor("scr_aT", [NE, 128, T], F32, kind=skind).ap()

    avail = (nc.sbuf_bytes_remaining - 512) // 32 * 32
    guard = nc.sbuf_tensor("arena", [128, avail], mybir.dt.uint8)
    big = guard.__enter__()
    nc._guards = [guard]

    def carve(off, cols, dt=F32):
        nb = cols * mybir.dt.size(dt)
        assert off % 32 == 0 and off + nb <= avail, ("overflow", off, nb, avail)
        return big[:, off : off + nb].bitcast(dt)

    class Bump:
        def __init__(self, start):
            self.off = start

        def take(self, cols, dt=F32):
            ap = carve(self.off, cols, dt)
            self.off = _al(self.off + cols * mybir.dt.size(dt))
            return ap

    cb = Bump(0)
    c_ident = cb.take(128)
    c_subln = cb.take(128)
    c_lam = cb.take(8)
    c_scl = cb.take(2)
    c_cst = cb.take(4)
    c_hsel = cb.take(2)
    c_one = cb.take(1)
    const_end = _al(cb.off)

    p1 = Bump(const_end)
    s_xT = p1.take(NE * T)
    s_wq = [p1.take(3 * E) for _ in range(2)]
    s_gb = [p1.take(3 * GR * 128) for _ in range(2)]
    s_spq = [p1.take(2 * GR * 128) for _ in range(2)]
    s_tmp = [p1.take(GR * 64) for _ in range(2)]
    s_cos = p1.take(NT * 64)
    s_sin = p1.take(NT * 64)
    assert p1.off <= avail, ("P1 overflow", p1.off, avail)

    p2 = Bump(const_end)
    s_qk = [p2.take(4 * T) for _ in range(2)]
    s_va = [p2.take(NT * 129) for _ in range(2)]
    s_E = [p2.take(2 * TQC * 128) for _ in range(3)]
    s_at = p2.take(NT * 128)
    s_aT = [p2.take(T) for _ in range(2)]
    s_ub = p2.take(9 * 129)
    s_uT = p2.take(2 * 4 * 128)
    s_zacc = [p2.take(2 * 4 * 128) for _ in range(2)]
    s_zw = p2.take(2 * 4 * 128)
    s_as = p2.take(TQC * 128)
    s_as2 = p2.take(TQC * 128)
    s_st = p2.take(64)
    s_zb = p2.take(NT * 4)
    s_zc = p2.take(NT * 2)
    s_zd = p2.take(NT * 2)
    s_dt = p2.take(max(16, 2 * NT))
    s_dstage = p2.take(128)
    s_qkbf = p2.take(4 * T, mybir.dt.bfloat16)
    assert p2.off <= avail, ("P2 overflow", p2.off, avail)
    s_lamrep = carve(s_gb[0].offset * 4, 256)
    s_lsc = carve(s_gb[0].offset * 4 + 1024, 64)

    p3 = Bump(const_end)
    s_y = p3.take(NT * E)
    s_d2 = p3.take(NE * E, FP8)
    p3b = Bump(p3.off)
    s_ga = [p3b.take(T) for _ in range(2)]
    s_gw = [p3b.take(E) for _ in range(2)]
    s_w2b = p3b.take(E)
    p3c = Bump(p3.off)
    s_d1 = p3c.take(NE * HTT * 128, FP8)
    s_sq = p3c.take(E)
    s_st3 = p3c.take(32)
    assert max(p3b.off, p3c.off) <= avail, ("P3 overflow", p3b.off, p3c.off)

    pg = nc.psum_tensor("pbig", [128, 8 * BANK], F32)
    pb = pg.__enter__()
    nc._guards.append(pg)

    def bank(i):
        return pb[:, i * BANK : (i + 1) * BANK]

    P = Plan()
    for s in ["IN", "W0", "W1", "Q0", "Q1", "SQ0", "SQ1", "SV0", "SV1",
              "SA0", "SA1", "G00", "G01", "G10", "G11", "W2S",
              "Y0", "Y1", "Y2", "Y3", "ST", "SD",
              "PE", "ACT", "DVE"]:
        P.sem(s)

    def din(fn, waits=(), sem="IN"):
        return P.emit("sync", fn, waits, sem, 16)

    def dout(fn, waits=(), sem="ST"):
        return P.emit("gpsimd", fn, waits, sem, 16)

    def pe(fn, waits=(), inc=True):
        return P.emit("tensor", fn, waits, "PE" if inc else None)

    def act(fn, waits=(), inc=True):
        return P.emit("scalar", fn, waits, "ACT" if inc else None)

    def dve(fn, waits=(), inc=True):
        return P.emit("vector", fn, waits, "DVE" if inc else None)

    nc.const_aps.register(F32, EPS, c_cst[:, 1:2])
    nc.const_aps.register(F32, PRE_BIAS, c_cst[:, 0:1])
    nc.const_aps.register(F32, 1e-30, c_cst[:, 2:3])
    dve(lambda e: e.memset(c_cst[:, 1:2], EPS))
    dve(lambda e: e.memset(c_cst[:, 0:1], PRE_BIAS))
    dve(lambda e: e.memset(c_cst[:, 2:3], 1e-30))
    P.drain("vector")
    din(lambda e: e.dma_start(c_ident, d_ident))
    din(lambda e: e.dma_start(c_subln, d_subln))
    din(lambda e: e.dma_start(s_lamrep, d_lam))
    din(lambda e: e.dma_start(c_scl, d_scl))
    din(lambda e: e.dma_start(c_hsel, d_hsel))
    din(lambda e: e.dma_start(s_cos, d_cos))
    din(lambda e: e.dma_start(s_sin, d_sin))
    for i in range(NE):
        din(lambda e, i=i: e.dma_start(s_xT[:, i * T : (i + 1) * T], d_xT[i]))
    ld_all = P.cnt["IN"]

    dve(lambda e: e.tensor_mul(s_lsc, s_lamrep[:, 0:64], s_lamrep[:, 64:128]),
        [("IN", ld_all)])
    P.drain("vector")
    dve(lambda e: e.reduce_sum(c_lam[:, 1:2], s_lsc, AxX))
    P.drain("vector")
    dve(lambda e: e.tensor_mul(s_lsc, s_lamrep[:, 128:192], s_lamrep[:, 192:256]))
    P.drain("vector")
    v_l = dve(lambda e: e.reduce_sum(c_lam[:, 2:3], s_lsc, AxX))
    dve(lambda e: e.reciprocal(c_lam[:, 7:8], c_scl[:, 0:1]))
    dve(lambda e: e.reciprocal(c_lam[:, 3:4], c_scl[:, 1:2]))
    P.drain("vector")
    v_rdq = dve(lambda e: e.tensor_mul(c_lam[:, 6:7], c_lam[:, 7:8], c_lam[:, 3:4]))
    a_l1 = act(lambda e: e.activation(c_lam[:, 4:5], c_lam[:, 1:2], Act.Exp),
               [("DVE", v_l)])
    a_l2 = act(lambda e: e.activation(c_lam[:, 5:6], c_lam[:, 2:3], Act.Exp))
    dve(lambda e: e.tensor_sub(c_lam[:, 1:2], c_lam[:, 4:5], c_lam[:, 5:6]),
        [("ACT", a_l2)])
    P.drain("vector")
    v_lam = dve(lambda e: e.tensor_scalar(c_lam[:, 0:1], c_lam[:, 1:2],
                                          LAMBDA_INIT, None, Alu.add))

    wring_n = [0, 0]
    for h in range(min(2, NH)):
        sl = h % 2
        din(lambda e, h=h, sl=sl: e.dma_start(s_wq[sl], d_wqkv[h]),
            sem="W%d" % sl)
        wring_n[sl] += 16

    evac_bank = {0: 0, 1: 0}
    gbuf_free = {0: ([], 0), 1: ([], 0)}
    spq_free = {0: 0, 1: 0}
    tbank_free = {0: 0, 1: 0}
    diag_bank_free = 0
    pendA = []
    pendB = []
    pe_head_end_p1 = {}
    GRB = GR * 128

    def stageA(h, grp, rope_cnt, gb_cnt):
        nonlocal_state = None
        gsl = (h * NGR + grp) % 2
        qb, kb = 2 + gsl, 4 + gsl
        w = [("DVE", rope_cnt), ("ACT", tbank_free[gsl])]
        for tl in range(GR):
            def tr(e, tl=tl, gsl=gsl, qb=qb):
                return e.transpose(
                    bank(qb)[:, tl * 128 : tl * 128 + 128],
                    s_gb[gsl][:, tl * 128 : tl * 128 + 128], c_ident)
            pe(tr, w if tl == 0 else (), inc=False)
        for tl in range(GR):
            def tr(e, tl=tl, gsl=gsl, kb=kb):
                return e.transpose(
                    bank(kb)[:, tl * 128 : tl * 128 + 128],
                    s_gb[gsl][:, GRB + tl * 128 : GRB + tl * 128 + 128], c_ident)
            pe(tr, (), inc=(tl == GR - 1))
        tr_cnt = P.cnt["PE"]
        ew = [("PE", tr_cnt), ("SQ%d" % gsl, spq_free[gsl])]
        act(lambda e, gsl=gsl, qb=qb: e.copy(
            s_spq[gsl][:, 0:GRB], bank(qb)[:, 0:GRB]), ew)
        se = act(lambda e, gsl=gsl, kb=kb: e.copy(
            s_spq[gsl][:, GRB : 2 * GRB], bank(kb)[:, 0:GRB]))
        tbank_free[gsl] = se
        c0 = grp * GRB
        dout(lambda e, h=h, gsl=gsl, c0=c0: e.dma_start(
            d_qT[h][0:64, c0 : c0 + GRB], s_spq[gsl][0:64, 0:GRB]),
            [("ACT", se)], sem="SQ%d" % gsl)
        dout(lambda e, h=h, gsl=gsl, c0=c0: e.dma_start(
            d_qT[h][0:64, T + c0 : T + c0 + GRB], s_spq[gsl][64:128, 0:GRB]),
            sem="SQ%d" % gsl)
        dout(lambda e, h=h, gsl=gsl, c0=c0: e.dma_start(
            d_kT[h][0:64, c0 : c0 + GRB], s_spq[gsl][0:64, GRB : 2 * GRB]),
            sem="SQ%d" % gsl)
        sq_val = dout(lambda e, h=h, gsl=gsl, c0=c0: e.dma_start(
            d_kT[h][0:64, T + c0 : T + c0 + GRB],
            s_spq[gsl][64:128, GRB : 2 * GRB]), sem="SQ%d" % gsl)
        spq_free[gsl] = sq_val
        sv_val = dout(lambda e, h=h, gsl=gsl, c0=c0: e.dma_start(
            d_v[h][:, c0 : c0 + GRB], s_gb[gsl][:, 2 * GRB : 3 * GRB]),
            [("ACT", gb_cnt)], sem="SV%d" % gsl)
        P.drain("vector")
        pr = dve(lambda e, gsl=gsl: e.tensor_mul(
            s_gb[gsl][:, 0:GRB], s_spq[gsl][:, 0:GRB],
            s_spq[gsl][:, GRB : 2 * GRB]), [("ACT", se)])
        return (h, grp, gsl, pr, tr_cnt, sv_val)

    def stageB(h, grp, gsl, pr, tr_cnt, sv_val):
        nonlocal diag_bank_free
        c0 = grp * GRB
        dg = pe(lambda e, gsl=gsl: e.matmul(
            pb[0:2, 6 * BANK : 6 * BANK + GRB], c_hsel, s_gb[gsl][:, 0:GRB],
            start=True, stop=True),
            [("DVE", pr), ("DVE", diag_bank_free)])
        P.drain("vector")
        mn = dve(lambda e, gsl=gsl: e.tensor_scalar(
            s_gb[gsl][0:2, GRB : GRB + GRB], pb[0:2, 6 * BANK : 6 * BANK + GRB],
            -1.0, -2.0, Alu.mult, Alu.min), [("PE", dg)])
        diag_bank_free = mn
        dout(lambda e, h=h, gsl=gsl, c0=c0: e.dma_start(
            d_qT[h][64:65, c0 : c0 + GRB], s_gb[gsl][0:1, GRB : GRB + GRB]),
            [("DVE", mn)], sem="SV%d" % gsl)
        sv2 = dout(lambda e, h=h, gsl=gsl, c0=c0: e.dma_start(
            d_qT[h][64:65, T + c0 : T + c0 + GRB],
            s_gb[gsl][1:2, GRB : GRB + GRB]), sem="SV%d" % gsl)
        gbuf_free[gsl] = ([("PE", dg)], sv2)

    for h in range(NH):
        wsl = h % 2
        wsem = "W%d" % wsl
        for grp in range(NGR):
            gsl = (h * NGR + grp) % 2
            if pendB:
                stageB(*pendB.pop(0))
            for tl in range(GR):
                tt = grp * GR + tl
                psl = (h * NT + tt) % 2
                mmw = [("IN", ld_all), (wsem, wring_n[wsl]),
                       ("ACT", evac_bank[psl])]
                for et in range(NE):
                    def mm(e, tt=tt, et=et, psl=psl, wsl=wsl):
                        lhsT = s_xT[:, et * T + tt * 128 : et * T + tt * 128 + 128]
                        rhs = s_wq[wsl].rearrange("p (s x) -> p s x", s=3)[
                            :, :, et * 128 : (et + 1) * 128]
                        return e.matmul(bank(psl)[:, 0:384].rearrange(
                            "p (s x) -> p s x", s=3), lhsT, rhs,
                            start=(et == 0), stop=(et == NE - 1))
                    pe(mm, mmw if et == 0 else (), inc=(et == NE - 1))
                mm_cnt = P.cnt["PE"]
                ew = [("PE", mm_cnt), ("DVE", v_lam)]
                if tl == 0:
                    ew = ew + gbuf_free[gsl][0]
                    if gbuf_free[gsl][1]:
                        ew = ew + [("SV%d" % gsl, gbuf_free[gsl][1])]
                def evac(e, tl=tl, gsl=gsl, psl=psl):
                    dst = s_gb[gsl].rearrange("p (s x) -> p s x", s=3)[
                        :, :, tl * 128 : (tl + 1) * 128]
                    return e.copy(dst, bank(psl)[:, 0:384].rearrange("p (s x) -> p s x", s=3))
                evac_bank[psl] = act(evac, ew)
            gb_cnt = P.cnt["ACT"]

            rope_w = [("ACT", gb_cnt), ("DVE", v_lam)]
            cosg = s_cos[:, grp * GR * 64 : (grp + 1) * GR * 64].rearrange(
                "p (g k) -> p g k", g=GR)
            sing = s_sin[:, grp * GR * 64 : (grp + 1) * GR * 64].rearrange(
                "p (g k) -> p g k", g=GR)
            P.drain("vector")
            for seg in range(2):
                segap = s_gb[gsl][:, seg * GRB : (seg + 1) * GRB]
                sg = segap.rearrange("p (g b) -> p g b", g=GR)
                evn, odd = sg[:, :, 0:128:2], sg[:, :, 1:128:2]
                tA = s_tmp[0].rearrange("p (g k) -> p g k", g=GR)
                tB = s_tmp[1].rearrange("p (g k) -> p g k", g=GR)
                dve(lambda e, odd=odd, tA=tA, sing=sing: e.tensor_mul(tA, odd, sing),
                    rope_w if seg == 0 else ())
                dve(lambda e, evn=evn, tB=tB, sing=sing: e.tensor_mul(tB, evn, sing))
                P.drain("vector")
                dve(lambda e, evn=evn, cosg=cosg: e.tensor_mul(evn, evn, cosg))
                dve(lambda e, odd=odd, cosg=cosg: e.tensor_mul(odd, odd, cosg))
                P.drain("vector")
                dve(lambda e, evn=evn, tA=tA: e.tensor_sub(evn, evn, tA))
                dve(lambda e, odd=odd, tB=tB: e.tensor_add(odd, odd, tB))
                P.drain("vector")
            rope_cnt = P.cnt["DVE"]

            pendA.append((h, grp, rope_cnt, gb_cnt))
            if len(pendA) > 1:
                pendB.append(stageA(*pendA.pop(0)))
        pe_head_end_p1[h] = P.cnt["PE"]
        if h + 2 < NH:
            sl2 = (h + 2) % 2
            din(lambda e, h2=h + 2, sl2=sl2: e.dma_start(s_wq[sl2], d_wqkv[h2]),
                [("PE", pe_head_end_p1[h])], sem="W%d" % sl2)
            wring_n[sl2] += 16
    while pendA:
        pendB.append(stageA(*pendA.pop(0)))
        while len(pendB) > 1:
            stageB(*pendB.pop(0))
    while pendB:
        stageB(*pendB.pop(0))

    bar1 = {s: P.cnt[s] for s in ("PE", "ACT", "DVE", "SQ0", "SQ1", "SV0", "SV1")}
    b1w = [(k, v) for k, v in bar1.items()]

    for sl in range(2):
        dve(lambda e, sl=sl: e.memset(
            s_va[sl].rearrange("p (t c) -> p t c", t=NT)[:, :, 128:129], 1.0),
            b1w)
        dve(lambda e, sl=sl: e.memset(s_qk[sl][64:66, 2 * T : 4 * T], 1.0))
    dve(lambda e: e.memset(c_one, 1.0))
    v_ones = P.cnt["DVE"]

    q_n = [0, 0]
    pe_head_end = {}
    sa_free = {0: 0, 1: 0}
    at_free_pe = 0
    tpack_free = {}
    slist = []
    clist = []
    sd_prev = 0
    devac_prev = 0
    cast_prev_pe = 0
    zmm_pe = {0: 0, 1: 0}
    dbg_gate = [0]

    def p2_load(h):
        sl = h % 2
        w = list(b1w)
        if h >= 2:
            w.append(("PE", pe_head_end[h - 2]))
        din(lambda e, h=h, sl=sl: e.dma_start(s_qk[sl][0:65, 0 : 2 * T], d_qT[h]),
            w + [("DVE", v_ones), ("ST", dbg_gate[0])], sem="Q%d" % sl)
        din(lambda e, h=h, sl=sl: e.dma_start(s_qk[sl][0:64, 2 * T : 4 * T], d_kT[h]),
            sem="Q%d" % sl)
        din(lambda e, h=h, sl=sl: e.dma_start(
            s_va[sl].rearrange("p (t c) -> p t c", t=NT)[:, :, 0:128],
            d_v[h].rearrange("p (t c) -> p t c", t=NT)),
            [("DVE", v_ones)], sem="Q%d" % sl)
        q_n[sl] += 48

    for h in range(min(2, NH)):
        p2_load(h)

    for h in range(NH):
        qsl = h % 2
        qsem = ("Q%d" % qsl, q_n[qsl])
        P.drain("vector")
        v_cast = dve(lambda e, qsl=qsl: e.tensor_copy(
            s_qkbf[0:66, :], s_qk[qsl][0:66, :]),
            [qsem, ("PE", cast_prev_pe)])
        ue_prev = 0
        at_out = {}
        sd_val = {0: 0}
        zlist = {}
        chunk_lastg = {}
        chunk_upe = {}

        def emit_scores(g, c, st):
            ssl = g % 2
            sw = [qsem]
            if g == 0 and h == 0:
                sw += b1w
            G = len(slist)
            if G >= 2:
                sw.append(("ACT", slist[G - 2]))
            if g < 2 and h > 0:
                sw.append(("ACT", tpack_free.get(2 * ssl, 0)))
                sw.append(("ACT", tpack_free.get(2 * ssl + 1, 0)))
            sw.append(("SD", sd_val[0]))
            for hf in range(2):
                def mm(e, c=c, st=st, ssl=ssl, hf=hf, qsl=qsl):
                    src = s_qk[qsl]
                    return e.matmul(
                        bank(2 * ssl + hf)[:, 0 : TQC * 128],
                        src[0:66, 2 * T + hf * T + st * 128 : 2 * T + hf * T + st * 128 + 128],
                        src[0:66, hf * T + c * TQC * 128 : hf * T + (c + 1) * TQC * 128],
                        start=True, stop=True)
                sc = pe(mm, sw if hf == 0 else (), inc=(hf == 1))
            xw = [("PE", sc), ("ST", dbg_gate[0])]
            if G >= 3:
                xw.append(("PE", clist[G - 3][0]))
                xw.append(("DVE", clist[G - 3][1]))
            sslot = G % 3
            def xp(e, ssl=ssl, sslot=sslot):
                src = pb[:, 2 * ssl * BANK : (2 * ssl + 2) * BANK].rearrange(
                    "p (b x) -> p b x", b=2)[:, :, 0 : TQC * 128]
                dst = s_E[sslot].rearrange("p (b x) -> p b x", b=2)
                return e.activation(dst, src, Act.Exp, bias=0.0, scale=SC_SCALE)
            e_cnt = act(xp, xw)
            slist.append(e_cnt)
            HW = TQC * 128
            zsl = c % 2
            P.drain("vector")
            if st == 0:
                zc = dve(lambda e, sslot=sslot, zsl=zsl: e.tensor_copy(
                    s_zacc[zsl][:, 0 : 2 * HW], s_E[sslot]),
                    [("ACT", e_cnt), ("PE", zmm_pe[zsl])])
            else:
                zc = dve(lambda e, sslot=sslot, zsl=zsl: e.tensor_add(
                    s_zacc[zsl][:, 0 : 2 * HW], s_zacc[zsl][:, 0 : 2 * HW],
                    s_E[sslot]), [("ACT", e_cnt)])
            zlist[g] = (G, e_cnt, zc)

        def emit_av(g, st):
            G2, e_cnt, zc = zlist[g]
            sslot = G2 % 3
            avw = [("ACT", e_cnt)]
            if st == 0:
                avw.append(("ACT", ue_prev))
            HW = TQC * 128
            for hf in range(2):
                def mavu(e, hf=hf, sslot=sslot, st=st, qsl=qsl, HW=HW):
                    return e.matmul(
                        pb[:, (4 + hf) * BANK : (4 + hf) * BANK + HW],
                        s_va[qsl][:, st * 129 : st * 129 + 128],
                        s_E[sslot][:, hf * HW : (hf + 1) * HW],
                        start=(st == 0), stop=(st == NT - 1))
                pe(mavu, avw if hf == 0 else (), inc=(hf == 1))
            if st == NT - 1:
                chunk_upe[len(chunk_upe)] = P.cnt["PE"]
            clist.append((P.cnt["PE"], zc))

        def pre_step(g, tt, hf, sc2):
            ssl = g % 2
            sw = [qsem, ("DVE", v_cast)]
            if g == 0 and h == 0:
                sw += b1w
            G = len(slist)
            if G >= 2:
                sw.append(("ACT", slist[G - 2]))
            if g < 2 and h > 0:
                sw.append(("ACT", tpack_free.get(2 * ssl, 0)))
                sw.append(("ACT", tpack_free.get(2 * ssl + 1, 0)))
            for hb in range(PB):
                def mm(e, tt=tt, hf=hf, sc2=sc2, hb=hb, ssl=ssl):
                    return e.matmul(
                        bank(2 * ssl + hb)[:, 0 : min(512, PW)],
                        s_qkbf[0:65, hf * T + tt * 128 : hf * T + tt * 128 + 128],
                        s_qkbf[0:65, 2 * T + hf * T + sc2 * PW + hb * 512 :
                               2 * T + hf * T + sc2 * PW + hb * 512 + min(512, PW)],
                        start=True, stop=True)
                sc = pe(mm, sw if hb == 0 else (), inc=(hb == PB - 1))
            sslot = G % 3
            xw = [("PE", sc), ("ST", dbg_gate[0])]
            if G >= 3:
                xw.append(("PE", clist[G - 3][0]))
                xw.append(("DVE", clist[G - 3][1]))
            def xp(e, tt=tt, hf=hf, sc2=sc2, ssl=ssl, sslot=sslot):
                src = pb[:, 2 * ssl * BANK : 2 * ssl * BANK + PW]
                dst = s_E[sslot][:, 0:PW]
                acc = s_zb[:, (tt * 2 + hf) * NSC2 + sc2 : (tt * 2 + hf) * NSC2 + sc2 + 1]
                return e.activation(dst, src, Act.Exp, bias=PRE_BIAS, scale=BETA,
                                    accum_out=acc)
            e_cnt = act(xp, xw)
            slist.append(e_cnt)
            clist.append((P.cnt["PE"], 0))

        def delta_head():
            nonlocal sd_prev, devac_prev
            z_act = slist[-1]
            P.drain("vector")
            v1 = dve(lambda e: e.tensor_reduce(
                s_zc, s_zb[:, 0 : 2 * NT * NSC2].rearrange("p (j k) -> p j k", k=NSC2), AxX, Alu.add),
                [("ACT", z_act)])
            a1 = act(lambda e: e.activation(s_zd, s_zc, Act.Ln, bias=1e-30),
                     [("DVE", v1)])
            P.drain("vector")
            def aff(e):
                src = bass.AP(s_zd.tensor, s_zd.offset,
                              [s_zd.ap[0], [1, 2], [2, NT]])
                dst = s_dt[:, 0 : 2 * NT]
                return e.tensor_scalar(dst, src, -1.0 / BETA,
                                       PRE_BIAS / BETA + MARGIN_RAW,
                                       Alu.mult, Alu.add)
            v2 = dve(aff, [("ACT", a1)])
            tr = pe(lambda e: e.transpose(
                pb[0 : 2 * NT, 7 * BANK : 7 * BANK + 128], s_dt[:, 0 : 2 * NT],
                c_ident), [("DVE", v2), ("ACT", devac_prev)])
            dv = act(lambda e: e.copy(s_dstage[0 : 2 * NT, 0:128],
                                      pb[0 : 2 * NT, 7 * BANK : 7 * BANK + 128]),
                     [("PE", tr), ("SD", sd_prev)])
            devac_prev = dv
            def dmad0(e, qsl=qsl):
                return e.dma_start(s_qk[qsl][65:66, 0:T], s_dstage[0:NT, 0:128])
            dout(dmad0, [("ACT", dv)], sem="SD")
            def dmad1(e, qsl=qsl):
                return e.dma_start(s_qk[qsl][65:66, T : 2 * T],
                                   s_dstage[NT : 2 * NT, 0:128])
            sd_prev = dout(dmad1, sem="SD")
            sd_val[0] = sd_prev

        tail_state = {}

        def tail_part1(c):
            HW = TQC * 128
            zw_mm = [("DVE", zlist[chunk_lastg[c]][2]), ("ACT", devac_prev)]
            for hf in range(2):
                def zmm(e, hf=hf, HW=HW, zsl=c % 2):
                    return e.matmul(
                        pb[0:1, (6 + hf) * BANK : (6 + hf) * BANK + HW],
                        c_one, s_zacc[zsl][:, hf * HW : (hf + 1) * HW],
                        start=True, stop=True)
                pe(zmm, zw_mm if hf == 0 else (), inc=(hf == 1))
            zmm_cnt = P.cnt["PE"]
            zmm_pe[c % 2] = zmm_cnt
            def zev(e, HW=HW):
                pap = pb[:]
                src = bass.AP(pap.tensor, pap.offset + 6 * BANK,
                              [[pap.ap[0][0], 1], [BANK, 2], [1, HW]])
                return e.copy(s_zw[0:1, 0 : 2 * HW].rearrange(
                    "p (b x) -> p b x", b=2), src)
            zev_cnt = act(zev, [("PE", zmm_cnt)])
            ztree = zev_cnt
            ut_w = [("PE", chunk_upe[c]), ("ST", dbg_gate[0])]
            for hf in range(2):
                def utev(e, hf=hf, HW=HW):
                    return e.copy(s_uT[:, hf * HW : (hf + 1) * HW],
                                  pb[:, (4 + hf) * BANK : (4 + hf) * BANK + HW])
                act(utev, ut_w if hf == 0 else ())
            utev_cnt = P.cnt["ACT"]
            tail_state[c] = (utev_cnt, ztree)

        def tail_part2(c):
            nonlocal ue_prev
            HW = TQC * 128
            utev_cnt, ztree = tail_state[c]
            tw = [("ACT", utev_cnt), ("ACT", ztree)]
            for a in range(2 * TQC):
                ub, us = 4 + a // 3, a % 3
                def trU(e, a=a, ub=ub, us=us):
                    return e.transpose(
                        pb[:, ub * BANK + us * 129 : ub * BANK + us * 129 + 128],
                        s_uT[:, a * 128 : (a + 1) * 128], c_ident)
                pe(trU, tw if a == 0 else ())
                def trZ(e, a=a, ub=ub, us=us):
                    return e.transpose(
                        pb[:, ub * BANK + us * 129 + 128 : ub * BANK + us * 129 + 129],
                        s_zw[0:1, a * 128 : (a + 1) * 128], c_ident[0:1, 0:1])
                pe(trZ, (), inc=(a == 2 * TQC - 1))
            tr_cnt2 = P.cnt["PE"]
            def uev(e):
                pap = pb[:]
                src = bass.AP(pap.tensor, pap.offset + 4 * BANK,
                              [pap.ap[0], [BANK, 2], [129, 3], [1, 129]])
                dst = s_ub[:, 0 : 6 * 129].rearrange("p (b s x) -> p b s x", b=2, s=3)
                return e.copy(dst, src)
            act(uev, [("PE", tr_cnt2), ("ST", dbg_gate[0])])
            def uev2(e):
                pap = pb[:]
                src = bass.AP(pap.tensor, pap.offset + 6 * BANK,
                              [pap.ap[0], [129, 2], [1, 129]])
                dst = s_ub[:, 6 * 129 : 8 * 129].rearrange("p (s x) -> p s x", s=2)
                return e.copy(dst, src)
            ue_prev = act(uev2)
            asw = [("ACT", ue_prev), ("DVE", v_lam), ("ST", dbg_gate[0])]
            if h > 0 and c == 0:
                asw.append(("PE", at_free_pe))
            P.drain("scalar")
            for sub in range(TQC):
                a0, a1 = sub, TQC + sub
                act(lambda e, sub=sub, a0=a0: e.activation(
                    s_st[:, 24 + 2 * sub : 25 + 2 * sub],
                    s_ub[:, a0 * 129 + 128 : a0 * 129 + 129], Act.Ln),
                    asw if sub == 0 else ())
                act(lambda e, sub=sub, a1=a1: e.activation(
                    s_st[:, 25 + 2 * sub : 26 + 2 * sub],
                    s_ub[:, a1 * 129 + 128 : a1 * 129 + 129], Act.Ln))
            P.drain("scalar")
            for sub in range(TQC):
                act(lambda e, sub=sub: e.activation(
                    s_st[:, 2 * sub : 2 * sub + 1],
                    s_st[:, 24 + 2 * sub : 25 + 2 * sub], Act.Exp, scale=-1.0))
                act(lambda e, sub=sub: e.activation(
                    s_st[:, 2 * sub + 1 : 2 * sub + 2],
                    s_st[:, 25 + 2 * sub : 26 + 2 * sub], Act.Exp, scale=-1.0))
            a_rz = P.cnt["ACT"]
            P.drain("vector")
            for sub in range(TQC):
                if sub == 0:
                    pass
                a1 = TQC + sub
                dve(lambda e, sub=sub, a1=a1: e.tensor_scalar(
                    s_as[:, sub * 128 : (sub + 1) * 128],
                    s_ub[:, a1 * 129 : a1 * 129 + 128],
                    s_st[:, 2 * sub + 1 : 2 * sub + 2], c_lam[:, 0:1],
                    Alu.mult, Alu.mult), [("ACT", a_rz)] if sub == 0 else ())
            P.drain("vector")
            for sub in range(TQC):
                a0 = sub
                dve(lambda e, sub=sub, a0=a0: e.scalar_tensor_tensor(
                    s_as2[:, sub * 128 : (sub + 1) * 128],
                    s_ub[:, a0 * 129 : a0 * 129 + 128],
                    s_st[:, 2 * sub : 2 * sub + 1],
                    s_as[:, sub * 128 : (sub + 1) * 128],
                    Alu.mult, Alu.subtract))
            P.drain("vector")
            for sub in range(TQC):
                dve(lambda e, sub=sub: e.tensor_mul(
                    s_as[:, sub * 128 : (sub + 1) * 128],
                    s_as2[:, sub * 128 : (sub + 1) * 128],
                    s_as2[:, sub * 128 : (sub + 1) * 128]))
            P.drain("vector")
            for sub in range(TQC):
                dve(lambda e, sub=sub: e.reduce_sum(
                    s_st[:, 8 + sub : 9 + sub],
                    s_as[:, sub * 128 : (sub + 1) * 128], AxX))
            v_red = P.cnt["DVE"]
            for sub in range(TQC):
                act(lambda e, sub=sub: e.activation(
                    s_st[:, 12 + sub : 13 + sub], s_st[:, 8 + sub : 9 + sub],
                    Act.Ln, bias=EPS, scale=1.0 / 128.0),
                    [("DVE", v_red)] if sub == 0 else ())
            P.drain("scalar")
            for sub in range(TQC):
                act(lambda e, sub=sub: e.activation(
                    s_st[:, 16 + sub : 17 + sub], s_st[:, 12 + sub : 13 + sub],
                    Act.Exp, scale=-0.5))
            a_rs = P.cnt["ACT"]
            for sub in range(TQC):
                dve(lambda e, sub=sub, c=c: e.scalar_tensor_tensor(
                    s_at[:, (c * TQC + sub) * 128 : (c * TQC + sub + 1) * 128],
                    s_as2[:, sub * 128 : (sub + 1) * 128],
                    s_st[:, 16 + sub : 17 + sub], c_subln,
                    Alu.mult, Alu.mult), [("ACT", a_rs)] if sub == 0 else ())
            at_out[c] = P.cnt["DVE"]
            if debug_scratch and h == 0 and c == NCH - 1:
                dout(lambda e, qsl=qsl: e.dma_start(d_dbg2[0][:, 0 : 4 * T], s_qk[qsl]),
                     [("SD", sd_val[0]), ("DVE", at_out[c])], sem="ST")
                dout(lambda e: e.dma_start(d_dbg2[1][:, 0 : 9 * 129], s_ub), sem="ST")
                dout(lambda e: e.dma_start(d_dbg2[2][:, 0 : NT * 128], s_at), sem="ST")
                dout(lambda e: e.dma_start(d_dbg2[3][:, 0 : 2 * TQC * 128], s_E[1]), sem="ST")
                dout(lambda e: e.dma_start(d_dbg2[4][:, 0:64], s_st), sem="ST")
                dout(lambda e: e.dma_start(d_dbg2[5][:, 0 : TQC * 128], s_as), sem="ST")
                dout(lambda e: e.dma_start(d_dbg2[6][:, 0 : TQC * 128], s_as2), sem="ST")
                dout(lambda e: e.dma_start(d_dbg2[7][:, 0:8], c_lam), sem="ST")
                dout(lambda e: e.dma_start(d_dbg2[7][:, 8:136], c_subln), sem="ST")
                dbg_gate[0] = P.cnt["ST"]

        g = 0
        if use_lse:
            for tt in range(NT):
                for hf in range(2):
                    for sc2 in range(NSC2):
                        pre_step(g, tt, hf, sc2)
                        g += 1
            delta_head()
        else:
            sd_val[0] = 0
        uq = []
        for c in range(NCH):
            for st in range(NT):
                if c > 0 and st == 3:
                    emit_av(*uq.pop(0))
                    emit_av(*uq.pop(0))
                emit_scores(g, c, st)
                uq.append((g, st))
                if st == NT - 1:
                    chunk_lastg[c] = g
                g += 1
                if c == 0:
                    if st >= 2:
                        emit_av(*uq.pop(0))
                elif st == 0:
                    emit_av(*uq.pop(0))
                elif st == 1:
                    emit_av(*uq.pop(0))
                    tail_part1(c - 1)
                elif st == 2:
                    tail_part2(c - 1)
                elif st >= 4:
                    emit_av(*uq.pop(0))
        emit_av(*uq.pop(0))
        emit_av(*uq.pop(0))
        tail_part1(NCH - 1)
        tail_part2(NCH - 1)
        cast_prev_pe = clist[-1][0]

        asl = h % 2
        last_exp = slist[-1]
        for pk in range(NT // 4):
            pkb = pk % 4
            cdone = min(NCH - 1, (4 * pk + 3) // TQC)
            tw = [("DVE", at_out[cdone]), ("ACT", last_exp),
                  ("ACT", tpack_free.get(pkb, 0))]
            for tl in range(4):
                tt = pk * 4 + tl
                pe(lambda e, tt=tt, tl=tl, pkb=pkb: e.transpose(
                    bank(pkb)[:, tl * 128 : (tl + 1) * 128],
                    s_at[:, tt * 128 : (tt + 1) * 128], c_ident),
                   tw if tl == 0 else (), inc=(tl == 3))
            tr = P.cnt["PE"]
            ew = [("PE", tr)]
            if pk == 0:
                ew.append(("SA%d" % asl, sa_free[asl]))
            tpack_free[pkb] = act(lambda e, pk=pk, asl=asl, pkb=pkb: e.copy(
                s_aT[asl][:, pk * 512 : pk * 512 + 512], bank(pkb)[:, 0:512]), ew)
        pe_head_end[h] = P.cnt["PE"]
        a_packs = P.cnt["ACT"]
        at_free_pe = pe_head_end[h]
        sa_free[asl] = dout(lambda e, h=h, asl=asl: e.dma_start(
            d_aT[h], s_aT[asl]), [("ACT", a_packs)], sem="SA%d" % asl)
        if h + 2 < NH:
            p2_load(h + 2)

    bar2 = {s: P.cnt[s] for s in ("PE", "ACT", "DVE", "SA0", "SA1")}
    b2w = [(k, v) for k, v in bar2.items()]

    gn_at = {}

    def g_loads(g, w):
        gp = g % 2
        et = g
        din(lambda e, et=et, gp=gp: e.dma_start(s_ga[gp], d_aT[et]),
            w, sem="G%d0" % gp)
        din(lambda e, et=et, gp=gp: e.dma_start(s_gw[gp], d_wout[et]),
            sem="G%d0" % gp)
        gn_at[g] = ("G%d0" % gp, P.cnt["G%d0" % gp])

    g_loads(0, list(b2w))
    g_loads(1, list(b2w))
    NYG = max(1, NT // 4)
    for tt in range(NT):
        din(lambda e, tt=tt: e.dma_start(s_y[:, tt * E : (tt + 1) * E], d_x[tt]),
            b2w, sem="Y%d" % (tt * 4 // NT))
    yin_done = [("Y%d" % i, P.cnt["Y%d" % i]) for i in range(4)]

    w2q = 0
    for kt in range(NE):
        din(lambda e, kt=kt: e.dma_start(s_w2b, d_w2[kt]),
            list(b2w) + ([("DVE", w2q)] if kt else []), sem="W2S")
        w2q = dve(lambda e, kt=kt: e.tensor_scalar(
            s_d2[:, kt * E : (kt + 1) * E], s_w2b, c_scl[:, 1:2], None, Alu.mult),
            [("W2S", 16 * (kt + 1))])

    wout_pe_grp = {}
    dve_add = {}
    for g in range(NWG):
        for tt in range(NT):
            tb = 4 * (tt % 2)
            mw = [gn_at[g],
                  ("DVE", dve_add.get((g, tt - 2), dve_add.get((g - 1, NT - 2 + tt), 0)))]
            for ch in range(ECH):
                def mwo(e, ch=ch, tt=tt, tb=tb, gp=g % 2):
                    return e.matmul(bank(tb + ch),
                             s_ga[gp][:, tt * 128 : (tt + 1) * 128],
                             s_gw[gp][:, ch * 512 : (ch + 1) * 512],
                             start=True, stop=True)
                pe(mwo, mw if ch == 0 else (), inc=(ch == ECH - 1))
            g_pe = P.cnt["PE"]
            aw = [("PE", g_pe)]
            if g == 0:
                aw.append(yin_done[tt * 4 // NT])
            dve_add[(g, tt)] = dve(lambda e, tt=tt, tb=tb: e.tensor_add(
                s_y[:, tt * E : (tt + 1) * E], s_y[:, tt * E : (tt + 1) * E],
                pb[:, tb * BANK : (tb + ECH) * BANK]), aw)
        wout_pe_grp[g] = P.cnt["PE"]
        if g + 2 < NWG:
            g_loads(g + 2, [("PE", wout_pe_grp[g])])

    wout_all_pe = P.cnt["PE"]
    g_done = [("G00", P.cnt["G00"]), ("G10", P.cnt["G10"]), ("W2S", P.cnt["W2S"]),
              ("DVE", w2q), ("PE", wout_all_pe)]

    dve_r2 = {}
    st_n = 0
    for half in range(NHALF):
        tts = range(half * HTT, (half + 1) * HTT)
        ln1 = {}
        for tt in tts:
            yt = s_y[:, tt * E : (tt + 1) * E]
            P.drain("vector")
            dve(lambda e, yt=yt: e.tensor_mul(s_sq, yt, yt), g_done)
            dve(lambda e, yt=yt: e.reduce_sum(s_st3[:, 0:1], yt, AxX))
            P.drain("vector")
            dve(lambda e: e.reduce_sum(s_st3[:, 1:2], s_sq, AxX))
            dve(lambda e: e.tensor_scalar(s_st3[:, 2:3], s_st3[:, 0:1],
                                          1.0 / E, None, Alu.mult))
            P.drain("vector")
            dve(lambda e: e.tensor_mul(s_st3[:, 3:4], s_st3[:, 2:3], s_st3[:, 2:3]))
            dve(lambda e: e.tensor_scalar(s_st3[:, 4:5], s_st3[:, 1:2],
                                          1.0 / E, None, Alu.mult))
            P.drain("vector")
            v_var = dve(lambda e: e.tensor_sub(s_st3[:, 5:6], s_st3[:, 4:5],
                                               s_st3[:, 3:4]))
            act(lambda e: e.activation(s_st3[:, 6:7], s_st3[:, 5:6], Act.Ln,
                                       bias=EPS), [("DVE", v_var)])
            P.drain("scalar")
            a_rs = act(lambda e: e.activation(s_st3[:, 7:8], s_st3[:, 6:7],
                                              Act.Exp, scale=-0.5))
            ln1[tt] = dve(lambda e, yt=yt: e.tensor_scalar(
                yt, yt, s_st3[:, 2:3], s_st3[:, 7:8], Alu.subtract, Alu.mult),
                [("ACT", a_rs)])
        dq_prev = {0: 0, 1: 0}
        r2_prev = max(dve_r2.values()) if dve_r2 else 0
        for tt in tts:
            tl = tt - half * HTT
            for pk in range(NE // 4):
                pkb = pk % 2
                tw = [("DVE", ln1[tt]), ("ACT", dq_prev[pkb]),
                      ("PE", wout_all_pe), ("DVE", r2_prev)]
                for q4 in range(4):
                    et2 = pk * 4 + q4
                    pe(lambda e, tt=tt, et2=et2, q4=q4, pkb=pkb: e.transpose(
                        bank(pkb)[:, q4 * 128 : (q4 + 1) * 128],
                        s_y[:, tt * E + et2 * 128 : tt * E + (et2 + 1) * 128],
                        c_ident), tw if q4 == 0 else (), inc=(q4 == 3))
                trc = P.cnt["PE"]
                def qev(e, pk=pk, tl=tl, pkb=pkb):
                    dst = bass.AP(s_d1.tensor, s_d1.offset + (pk * 4 * HTT + tl) * 128,
                                  [s_d1.ap[0], [HTT * 128, 4], [1, 128]])
                    return e.activation(dst, bank(pkb)[:, 0:512], Act.Copy,
                                 scale=c_scl[:, 0:1], bias=0.0)
                dq_prev[pkb] = act(qev, [("PE", trc)])
        d1_done = P.cnt["ACT"]
        for tt in tts:
            tl = tt - half * HTT
            tb = 4 * (tl % 2)
            fw = [("ACT", d1_done), ("DVE", dve_r2.get(tt - 2, r2_prev))]
            for ch in range(ECH):
                for kp in range(NE // 2):
                    def mf(e, tl=tl, ch=ch, kp=kp, tb=tb):
                        lw = bass.AP(s_d1.tensor,
                                     s_d1.offset + 2 * kp * HTT * 128 + tl * 128,
                                     [s_d1.ap[0], [HTT * 128, 2], [1, 128]])
                        rx = bass.AP(s_d2.tensor,
                                     s_d2.offset + 2 * kp * E + ch * 512,
                                     [s_d2.ap[0], [E, 2], [1, 512]])
                        return e.matmul(bank(tb + ch), lw, rx,
                                 start=(kp == 0), stop=(kp == NE // 2 - 1),
                                 perf_mode=mybir.MatmulPerfMode.DoubleRow)
                    pe(mf, fw if (ch == 0 and kp == 0) else (),
                       inc=(ch == ECH - 1 and kp == NE // 2 - 1))
            f_pe = P.cnt["PE"]
            yt = s_y[:, tt * E : (tt + 1) * E]
            P.drain("vector")
            dve(lambda e, yt=yt, tb=tb: e.scalar_tensor_tensor(
                yt, pb[:, tb * BANK : (tb + ECH) * BANK], c_lam[:, 6:7], yt,
                Alu.mult, Alu.add), [("PE", f_pe), ("DVE", v_rdq)])
            P.drain("vector")
            dve(lambda e, yt=yt: e.tensor_mul(s_sq, yt, yt))
            dve(lambda e, yt=yt: e.reduce_sum(s_st3[:, 0:1], yt, AxX))
            P.drain("vector")
            dve(lambda e: e.reduce_sum(s_st3[:, 1:2], s_sq, AxX))
            dve(lambda e: e.tensor_scalar(s_st3[:, 2:3], s_st3[:, 0:1],
                                          1.0 / E, None, Alu.mult))
            P.drain("vector")
            dve(lambda e: e.tensor_mul(s_st3[:, 3:4], s_st3[:, 2:3], s_st3[:, 2:3]))
            dve(lambda e: e.tensor_scalar(s_st3[:, 4:5], s_st3[:, 1:2],
                                          1.0 / E, None, Alu.mult))
            P.drain("vector")
            v_var = dve(lambda e: e.tensor_sub(s_st3[:, 5:6], s_st3[:, 4:5],
                                               s_st3[:, 3:4]))
            act(lambda e: e.activation(s_st3[:, 6:7], s_st3[:, 5:6], Act.Ln,
                                       bias=EPS), [("DVE", v_var)])
            P.drain("scalar")
            a_rs = act(lambda e: e.activation(s_st3[:, 7:8], s_st3[:, 6:7],
                                              Act.Exp, scale=-0.5))
            dve_r2[tt] = dve(lambda e, yt=yt: e.tensor_scalar(
                yt, yt, s_st3[:, 2:3], s_st3[:, 7:8], Alu.subtract, Alu.mult),
                [("ACT", a_rs)])
            dout(lambda e, tt=tt: e.dma_start(d_out[tt], s_y[:, tt * E : (tt + 1) * E]),
                 [("DVE", dve_r2[tt])])
            st_n += 16
    if debug_scratch:
        d_dbg = nc.dram_tensor("dbg", [8, 128, 4 * T], F32, kind="ExternalOutput").ap()
        enda = [("DVE", P.cnt["DVE"]), ("ACT", P.cnt["ACT"]), ("PE", P.cnt["PE"])]
        dout(lambda e: e.dma_start(d_dbg[0][:, 0 : NT * 128], s_at), enda, sem="ST")
        st_n += 16
        dout(lambda e: e.dma_start(d_dbg[1][:, 0 : 9 * 129], s_ub), sem="ST")
        st_n += 16
        dout(lambda e: e.dma_start(d_dbg[2], s_qk[0]), sem="ST")
        st_n += 16
        dout(lambda e: e.dma_start(d_dbg[3], s_qk[1]), sem="ST")
        st_n += 16
        dout(lambda e: e.dma_start(d_dbg[4][:, 0 : 2 * TQC * 128], s_E[0]), sem="ST")
        st_n += 16
        dout(lambda e: e.dma_start(d_dbg[5][:, 0 : 2 * TQC * 128], s_E[1]), sem="ST")
        st_n += 16
        dout(lambda e: e.dma_start(d_dbg[6][:, 0 : NT * 129], s_va[1]), sem="ST")
        st_n += 16
    P.emit("gpsimd", None, [("ST", P.cnt["ST"])])

    semh = {}
    import contextlib
    stack = contextlib.ExitStack()
    nc._semstack = stack
    for s in P.cnt:
        semh[s] = stack.enter_context(nc.semaphore(s))

    with nc.Block() as blk:
        def runner(eng_obj, name):
            for fn, waits, inc, inc_n in P.ops[name]:
                for s, v in waits:
                    eng_obj.wait_ge(semh[s], v)
                if fn == "drain":
                    eng_obj.drain()
                    continue
                if fn is None:
                    continue
                inst = fn(eng_obj)
                if inc is not None and inst is not None:
                    inst.then_inc(semh[inc], inc_n)

        blk.sync(lambda e: runner(e, "sync"))
        blk.gpsimd(lambda e: runner(e, "gpsimd"))
        blk.tensor(lambda e: runner(e, "tensor"))
        blk.scalar(lambda e: runner(e, "scalar"))
        blk.vector(lambda e: runner(e, "vector"))

    return nc



def stage_shared(wq, wk, wv, wout, w2, lq1, lq2, lk1, lk2, subln, scale, scale1,
                 NT=16, NH=16):
    NE = NH
    T, E = 128 * NT, 128 * NE
    f32 = np.float32

    def whead(w):
        r = w.reshape(NH, 128, NE, 128)
        return np.ascontiguousarray(r.transpose(0, 3, 2, 1)).reshape(NH, 128, E)

    wqkv = np.ascontiguousarray(
        np.concatenate([whead(wq), whead(wk), whead(wv)], axis=2))
    wout_img = np.ascontiguousarray(wout.T.reshape(NE, 128, E))
    w2_img = np.ascontiguousarray(w2.reshape(NE, 128, E))

    inv = 1.0 / 10000.0 ** (np.arange(0, 64, 2)[:32].astype(np.float64) / 64)
    ang = np.outer(np.arange(T), inv)
    cos = np.cos(ang).astype(f32)
    sin = np.sin(ang).astype(f32)
    cos_img = np.ascontiguousarray(
        np.tile(cos, (1, 2)).reshape(NT, 128, 64).transpose(1, 0, 2).reshape(128, NT * 64))
    sin_img = np.ascontiguousarray(
        np.tile(sin, (1, 2)).reshape(NT, 128, 64).transpose(1, 0, 2).reshape(128, NT * 64))

    ident = np.eye(128, dtype=f32)
    subln_img = np.ascontiguousarray(np.broadcast_to(subln[None, :], (128, 128)))
    lam_img = np.ascontiguousarray(np.broadcast_to(
        np.concatenate([lq1, lk1, lq2, lk2])[None, :], (128, 256)))
    scl_img = np.ascontiguousarray(np.broadcast_to(
        np.stack([np.float32(scale), np.float32(scale1)])[None, :], (128, 2)))
    hsel = np.zeros((128, 2), dtype=f32)
    hsel[0:64, 0] = 1.0
    hsel[64:128, 1] = 1.0
    return dict(wqkv=wqkv, wout=wout_img, w2=w2_img, cos=cos_img, sin=sin_img,
                ident=ident, subln=subln_img, lam=lam_img, scl=scl_img,
                hsel=hsel)


def stage_x(xb, NT=16, NH=16):
    NE = NH
    T, E = 128 * NT, 128 * NE
    xT = np.ascontiguousarray(xb.T.reshape(NE, 128, T))
    xi = np.ascontiguousarray(xb.reshape(NT, 128, E))
    return xT, xi


def kernel(L0_attn_x, L0_attn_wq, L0_attn_wk, L0_attn_wv, L0_attn_wout,
           L0_attn_lambda_q1, L0_attn_lambda_q2, L0_attn_lambda_k1,
           L0_attn_lambda_k2, L0_attn_subln_weight, L0_ffn_input2,
           L0_ffn_scale, L0_ffn_scale1):
    NT = NH = 16
    x = np.ascontiguousarray(np.asarray(L0_attn_x, np.float32))
    shared = stage_shared(
        np.asarray(L0_attn_wq, np.float32), np.asarray(L0_attn_wk, np.float32),
        np.asarray(L0_attn_wv, np.float32), np.asarray(L0_attn_wout, np.float32),
        np.asarray(L0_ffn_input2, np.float32),
        np.asarray(L0_attn_lambda_q1, np.float32),
        np.asarray(L0_attn_lambda_q2, np.float32),
        np.asarray(L0_attn_lambda_k1, np.float32),
        np.asarray(L0_attn_lambda_k2, np.float32),
        np.asarray(L0_attn_subln_weight, np.float32),
        np.asarray(L0_ffn_scale, np.float32), np.asarray(L0_ffn_scale1, np.float32),
        NT, NH)
    in_maps = []
    for b in range(8):
        xT, xi = stage_x(x[b], NT, NH)
        m = dict(shared)
        m["xT"] = xT
        m["x"] = xi
        in_maps.append(m)
    nc = build_program(NT, NH)
    res = run_bass_kernel_spmd(nc, in_maps, core_ids=list(range(8)))
    outs = [np.asarray(r["out"]).reshape(2048, 2048) for r in res.results]
    return np.stack(outs, axis=0)
